# version 17
# baseline (speedup 1.0000x reference)
import contextlib

import numpy as np
import ml_dtypes

import concourse.bacc as bacc
import concourse.bass as bass
import concourse.mybir as mybir
from concourse.bass_utils import run_bass_kernel_spmd
from concourse.hw_specs import get_activation_tables

F32 = mybir.dt.float32
BF16 = mybir.dt.bfloat16
I32 = mybir.dt.int32
AF = mybir.ActivationFunctionType
OP = mybir.AluOpType
AX = mybir.AxisListType

B, S, E = 4, 4096, 1024
TOWN = 2048
NCORES = 8
LN2C = float(np.log(2.0))
INVLN2 = float(1.0 / np.log(2.0))

DBG = False

SB_BASE = 16512
SMALL_SZ = 16192
BIG1 = SB_BASE + SMALL_SZ
BIG2 = BIG1 + 32768
BIG3 = BIG2 + 32768
BIG4 = BIG3 + 32768
BIG5 = BIG4 + 32768
assert BIG5 + 65536 <= 229344


class BK:
    def __init__(self):
        self.c = {}
        self.h = {}

    def add(self, name, handle):
        self.h[name] = handle
        self.c[name] = 0

    def inc(self, inst, name, n=1):
        inst.then_inc(self.h[name], n)
        self.c[name] += n
        return self.c[name]

    def w(self, eng, name, val=None):
        eng.wait_ge(self.h[name], self.c[name] if val is None else val)

    def v(self, name):
        return self.c[name]

    def chain(self, eng, inst, name):
        self.inc(inst, name, 1)
        eng.wait_ge(self.h[name], self.c[name])


def build_nc():
    nc = bacc.Bacc("TRN2", target_bir_lowering=False)
    ACT_SET_LNEXP = next(
        i for i, fs in enumerate(get_activation_tables(nc.m.arch).values())
        if AF.Ln in fs and AF.Exp in fs)

    def din(name, shape, dtype):
        return nc.dram_tensor(name, shape, dtype, kind="ExternalInput").ap()

    x0f = din("x0f", [E, S], F32)
    x0o = din("x0o", [E, TOWN], F32)
    wq0 = din("wq0", [8, 64, 64], F32)
    wk0 = din("wk0", [8, 64, 64], F32)
    wv0 = din("wv0", [8, 64, 64], F32)
    wo0 = din("wo0", [E, E], F32)
    wf0 = din("wf0", [E, E], F32)
    bf0 = din("bf0", [128, 8], F32)
    wq1 = din("wq1", [E, 512], F32)
    wk1 = din("wk1", [E, 512], F32)
    wv1 = din("wv1", [E, 512], F32)
    wo1 = din("wo1", [E, E], F32)
    bo1 = din("bo1", [128, 8], F32)
    uRd = din("uR", [4, 512, 512], F32)
    vRd = din("vR", [4, 512, 512], F32)
    uId = din("uI", [4, 512, 512], F32)
    vId = din("vI", [4, 512, 512], F32)
    bfc = din("bfc", [128, 1024], F32)
    idb = din("idb", [128, 128], BF16)
    idf = din("idf", [128, 128], F32)
    mbig = din("mbig", [128, 4096], BF16)
    selh = din("selh", [128, 4], BF16)
    cst = din("cst", [128, 8], F32)
    out = nc.dram_tensor("out", [TOWN, E], F32, kind="ExternalOutput").ap()
    if DBG:
        dbg_ln0 = nc.dram_tensor("dbg_ln0", [128, 16384], BF16, kind="ExternalOutput").ap()
        dbg_cat = nc.dram_tensor("dbg_cat", [128, 16384], BF16, kind="ExternalOutput").ap()
        dbg_scrm = nc.dram_tensor("dbg_scrm", [128, 16384], BF16, kind="ExternalOutput").ap()
        dbg_x1 = nc.dram_tensor("dbg_x1", [128, 16384], F32, kind="ExternalOutput").ap()
        dbg_x2 = nc.dram_tensor("dbg_x2", [128, 16384], F32, kind="ExternalOutput").ap()
        dbg_x3 = nc.dram_tensor("dbg_x3", [128, 16384], F32, kind="ExternalOutput").ap()
        dbg_conc = nc.dram_tensor("dbg_conc", [128, 16384], BF16, kind="ExternalOutput").ap()
        dbg_qfm = nc.dram_tensor("dbg_qfm", [128, 16384], BF16, kind="ExternalOutput").ap()
        dbg_kfm = nc.dram_tensor("dbg_kfm", [128, 16384], BF16, kind="ExternalOutput").ap()
        dbg_vtm = nc.dram_tensor("dbg_vtm", [128, 16380], BF16, kind="ExternalOutput").ap()
        dbg_xq = nc.dram_tensor("dbg_xq", [128, 8192], BF16, kind="ExternalOutput").ap()
        dbg_rdt = nc.dram_tensor("dbg_rdt", [128, 256], F32, kind="ExternalOutput").ap()
        dbg_wqt = nc.dram_tensor("dbg_wqt", [128, 8192], BF16, kind="ExternalOutput").ap()
        dbg_sc = nc.dram_tensor("dbg_sc", [128, 4], F32, kind="ExternalOutput").ap()
        dbg_ln1 = nc.dram_tensor("dbg_ln1", [8, 128, 2048], F32, kind="ExternalOutput").ap()
        dbg_ln3 = nc.dram_tensor("dbg_ln3", [128, 16384], BF16, kind="ExternalOutput").ap()
        dbg_r12 = nc.dram_tensor("dbg_r12", [128, 8192], BF16, kind="ExternalOutput").ap()
        dbg_x3t = nc.dram_tensor("dbg_x3t", [128, 2048], F32, kind="ExternalOutput").ap()

    x2sp = nc.dram_tensor("x2sp", [E, TOWN], F32)
    ln1sp = nc.dram_tensor("ln1sp", [8, 128, TOWN], F32)
    r12sp = nc.dram_tensor("r12sp", [2, 8, 128, 512], BF16)
    am_in = nc.dram_tensor("am_in", [1, 8], F32)
    am_out = nc.dram_tensor("am_out", [1, 8], F32)
    agl_in = nc.dram_tensor("agl_in", [E, TOWN], BF16)
    agl_out = nc.dram_tensor("agl_out", [2 * E, TOWN], BF16)
    agc_in = nc.dram_tensor("agc_in", [512, TOWN], BF16)
    agc_out = nc.dram_tensor("agc_out", [1024, TOWN], BF16)

    PAIRS = [[0, 1], [2, 3], [4, 5], [6, 7]]
    bk = BK()
    EV = {}

    def fm3(dram2d):
        return dram2d.rearrange("(c p) t -> p c t", p=128)

    def sb3(sbslice, t):
        return sbslice.rearrange("p (c t) -> p c t", t=t)

    def sb(name, shape, dtype, off):
        return nc.alloc_sbuf_tensor_at(name, shape, dtype, offset=off)

    o = SB_BASE

    def nxt(n):
        nonlocal o
        r = o
        o += (n + 31) // 32 * 32
        return r

    IDB = sb("IDB", [128, 128], BF16, nxt(256))
    IDF = sb("IDF", [128, 128], F32, nxt(512))
    SELH = sb("SELH", [128, 4], BF16, nxt(32))
    VTX = sb("VTX", [128, 260], BF16, nxt(520))
    ONESR = sb("ONESR", [65, 64], F32, nxt(256))
    CST = sb("CST", [128, 8], F32, nxt(32))
    BF0 = sb("BF0", [128, 8], F32, nxt(32))
    BO1 = sb("BO1", [128, 8], F32, nxt(32))
    ONESF = sb("ONESF", [128, 1], F32, nxt(32))
    ONESB = sb("ONESB", [128, 1], BF16, nxt(32))
    ONES1R = sb("ONES1R", [1, 128], F32, nxt(512))
    WQ0 = sb("WQ0", [128, 256], BF16, nxt(512))
    WK0 = sb("WK0", [128, 256], BF16, nxt(512))
    WVD0 = sb("WVD0", [128, 512], BF16, nxt(1024))
    BIASM = sb("BIASM", [128, 4], F32, nxt(32))
    XBF = sb("XBF", [128, 1024], BF16, nxt(2048))
    XSQ = sb("XSQ", [128, 1024], BF16, nxt(2048))
    STATA = sb("STATA", [33, 512], F32, nxt(2048))
    STATB = sb("STATB", [33, 512], F32, nxt(2048))
    NRMC = sb("NRMC", [34, 16], F32, nxt(64))
    NRT = sb("NRT", [1, 64], F32, nxt(256))
    AMXC = sb("AMXC", [128, 32], F32, nxt(128))
    SCR1 = sb("SCR1", [128, 512], F32, nxt(2048))
    SCAL = sb("SCAL", [1, 8], F32, nxt(32))
    SCXV = sb("SCXV", [128, 1], F32, nxt(32))
    SCWV = sb("SCWV", [128, 1], F32, nxt(32))
    DSCV = sb("DSCV", [128, 1], F32, nxt(32))
    SC1 = sb("SC1", [1, 16], F32, nxt(64))
    assert o <= BIG1, o

    QSQ = sb("QSQ", [128, 512], BF16, BIG1)
    BFC = sb("BFC", [128, 1024], BF16, BIG2)
    PT = sb("PT", [128, 8192], BF16, BIG5 + 32768)
    RDT = sb("RDT", [128, 256], F32, BIG5 + 49152)
    RDTL = sb("RDTL", [128, 256], F32, BIG5 + 50176)
    S1D = sb("S1D", [65, 1024], F32, BIG5 + 49152)
    RDD = sb("RDD", [65, 1024], F32, BIG5 + 53248)
    RREPS = sb("RREPS", [128, 512], F32, BIG5 + 14336)
    MBIG = sb("MBIG", [128, 4096], BF16, BIG5 + 57344)
    LN0 = sb("LN0", [128, 16384], BF16, BIG1)
    CAT = sb("CAT", [128, 16384], BF16, BIG1)
    X0OR = sb("X0OR", [128, 8192], F32, BIG1)
    CONC = sb("CONC", [128, 16384], BF16, BIG1)
    WOUT1 = sb("WOUT1", [128, 8192], BF16, BIG1)

    QFM = sb("QFM", [128, 16384], BF16, BIG2)
    SCRM = sb("SCRM", [128, 16384], BF16, BIG2)

    KFM = sb("KFM", [128, 16384], BF16, BIG3)
    WF32 = sb("WF32", [128, 8192], F32, BIG3)
    WQT = sb("WQT", [128, 8192], BF16, BIG3)
    LN1R = sb("LN1R", [128, 1024], F32, BIG3 + 16384)
    LN1T = sb("LN1T", [128, 1024], F32, BIG3 + 16384 + 4096)
    LN2OWN = sb("LN2OWN", [128, 16384], BF16, BIG3)
    CONCR = sb("CONCR", [128, 8192], BF16, BIG3)
    CONCO = sb("CONCO", [128, 8192], BF16, BIG3 + 16384)
    LN3 = sb("LN3", [128, 16384], BF16, BIG3)

    VTM = sb("VTM", [128, 16380], BF16, BIG4)

    def vsl(kbd, hp, h):
        gi = kbd * 8 + hp * 2 + h
        if gi < 252:
            return VTM[:, gi * 65:(gi + 1) * 65]
        return VTX[:, (gi - 252) * 65:(gi - 251) * 65]

    def vout(gi0, n):
        if gi0 < 252:
            base = VTM[:, gi0 * 65:(gi0 + n) * 65]
        else:
            base = VTX[:, (gi0 - 252) * 65:(gi0 - 252 + n) * 65]
        return base.rearrange("p (c t) -> p c t", t=65)[:, :, 0:64]
    WOB = sb("WOB", [128, 8192], BF16, BIG4)
    QA = sb("QA", [128, 512], F32, BIG4 + 16384)
    QZ = sb("QZ", [128, 512], F32, BIG4 + 16384 + 2048)
    QE = sb("QE", [128, 512], I32, BIG4 + 16384 + 4096)
    QT = sb("QT", [128, 512], I32, BIG4 + 16384 + 6144)
    QM = sb("QM", [128, 512], I32, BIG4 + 16384 + 8192)
    QTC = sb("QTC", [128, 512], F32, BIG4 + 16384 + 10240)
    QSG = sb("QSG", [128, 512], F32, BIG4 + 16384 + 12288)
    ONESI = sb("ONESI", [128, 512], I32, BIG4 + 16384 + 14336)
    QA1 = sb("QA1", [128, 512], F32, BIG1)
    QZ1 = sb("QZ1", [128, 512], F32, BIG1 + 2048)
    QE1 = sb("QE1", [128, 512], I32, BIG1 + 4096)
    QSG1 = sb("QSG1", [128, 512], F32, BIG1 + 6144)
    XQR = sb("XQR", [128, 8192], BF16, BIG4)
    R12 = sb("R12", [128, 8192], BF16, BIG4)
    X3TM = sb("X3TM", [128, 2048], F32, BIG4 + 16384)
    X4T = sb("X4T", [128, 2048], F32, BIG4 + 16384 + 8192)

    X0FR = sb("X0FR", [128, 8192], F32, BIG5)
    LN2FR = sb("LN2FR", [128, 8192], BF16, BIG5)
    WQKV1 = sb("WQKV1", [128, 12288], BF16, BIG5 + 16384)
    STRM = sb("STRM", [128, 16384], F32, BIG5)
    UVU = sb("UVU", [128, 1024], F32, BIG5 + 32768)
    UVV = sb("UVV", [128, 1024], F32, BIG5 + 32768 + 4096)
    WTACC = sb("WTACC", [128, 2048], F32, BIG5 + 32768 + 8192)
    R12B = sb("R12B", [128, 8192], BF16, BIG5 + 49152)

    ps_sc0 = nc.alloc_psum_tensor("ps_sc0", [128, 1024], F32)
    ps_sc1 = nc.alloc_psum_tensor("ps_sc1", [128, 1024], F32)
    ps_avd = nc.alloc_psum_tensor("ps_avd", [128, 1024], F32)
    ps_av = ps_avd[:, 0:512]
    ps_den = ps_avd[:, 512:1024]
    ps_m1 = nc.alloc_psum_tensor("ps_m1", [128, 512], F32)
    ps_tb = nc.alloc_psum_tensor("ps_tb", [128, 1024], BF16)

    stack = contextlib.ExitStack()
    for name in ["d0", "dx0", "dx1", "d_rl", "d_r12", "d_oa", "d_ob", "d_mb",
                 "d_mb2", "d_wk", "d_uv0", "d_uv1", "d_rsp", "d_wf", "d_sp0",
                 "d_sp1", "d_xsp", "d_wo", "d_gw", "d_am", "d_p10", "d_bfc",
                 "cc", "sT", "sS", "sV", "sG"]:
        bk.add(name, stack.enter_context(nc.semaphore(name)))
    block = stack.enter_context(nc.Block())

    TE = block.tensor

    def SE(fn):
        def _wrap(s):
            bk.w(s, "sS")
            fn(s)
        block.scalar(_wrap)

    def VE(fn):
        def _wrap(v):
            bk.w(v, "sV")
            fn(v)
        block.vector(_wrap)

    GE = block.gpsimd
    SY = block.sync

    def dump(name_ap, src_sb):
        if not DBG:
            return
        def _d(sy, name_ap=name_ap, src_sb=src_sb):
            bk.w(sy, "sV")
            bk.w(sy, "sS")
            bk.w(sy, "sT")
            bk.inc(sy.dma_start(out=name_ap, in_=src_sb), "d0", 16)
            bk.w(sy, "d0")
        SY(_d)
        gate = bk.v("d0")
        VE(lambda v: bk.w(v, "d0", gate))
        SE(lambda s: bk.w(s, "d0", gate))
        TE(lambda t: bk.w(t, "d0", gate))

        def _g(g, gate=gate):
            g.wait_ge(bk.h["d0"], gate)
        GE(_g)

    if DBG:
        EVD = {}

    def p0(sy):
        for dst, src in [
            (IDB[:, :], idb), (IDF[:, :], idf),
            (SELH[:, :], selh), (CST[:, :], cst),
            (BF0[:, :], bf0), (BO1[:, :], bo1),
        ]:
            bk.inc(sy.dma_start(out=dst, in_=src), "d0", 16)
        EV["p0_sy"] = bk.v("d0")
    SY(p0)

    def s_pretbl(s):
        inst = s.add_instruction(mybir.InstLoadActFuncSet(
            name=nc.get_next_instruction_name(),
            act_func_set_id=ACT_SET_LNEXP, ins=[], outs=[]))
        bk.inc(inst, "sS", 1)
    SE(s_pretbl)

    def v_ones(v):
        v.memset(ONESR[:, :], 1.0)
        v.memset(VTM[:, :].rearrange("p (c t) -> p c t", t=65)[:, :, 64:65], 1.0)
        inst = v.memset(VTX[:, :].rearrange("p (c t) -> p c t", t=65)[:, :, 64:65], 1.0)
        bk.inc(inst, "sV", 1)
        EV["vones0"] = bk.v("sV")
    VE(v_ones)

    def p0g(g):
        g.memset(ONESF[:, :], 1.0)
        g.memset(ONESB[:, :], 1.0)
        g.memset(ONES1R[:, :], 1.0)
        inst = g.memset(WVD0[:, :], 0.0)
        bk.inc(inst, "sG", 1)
        g.wait_ge(bk.h["sG"], bk.v("sG"))
        for hp in range(4):
            bk.inc(g.dma_start(out=WQ0[0:64, hp * 64:hp * 64 + 64], in_=wq0[2 * hp]), "d_gw", 16)
            bk.inc(g.dma_start(out=WQ0[64:128, hp * 64:hp * 64 + 64], in_=wq0[2 * hp + 1]), "d_gw", 16)
            bk.inc(g.dma_start(out=WK0[0:64, hp * 64:hp * 64 + 64], in_=wk0[2 * hp]), "d_gw", 16)
            bk.inc(g.dma_start(out=WK0[64:128, hp * 64:hp * 64 + 64], in_=wk0[2 * hp + 1]), "d_gw", 16)
            bk.inc(g.dma_start(out=WVD0[0:64, hp * 128:hp * 128 + 64], in_=wv0[2 * hp]), "d_gw", 16)
            bk.inc(g.dma_start(out=WVD0[64:128, hp * 128 + 64:hp * 128 + 128], in_=wv0[2 * hp + 1]), "d_gw", 16)
        inst = g.nop()
        bk.inc(inst, "sG", 1)
        EV["p0_g"] = bk.v("sG")
        EV["p0_gd"] = bk.v("d_gw")
    GE(p0g)

    def ln_stats(tag, st, src_of, ev_pre):
        for c in range(8):
            cs = c % 2

            def sc_(s, c=c, cs=cs):
                if c == 0 and ev_pre is not None:
                    bk.w(s, ev_pre[0], ev_pre[1])
                if f"{tag}_smm_{st}_{c-2}" in EV:
                    bk.w(s, "sT", EV[f"{tag}_smm_{st}_{c-2}"])
                bk.chain(s, s.activation(out=XBF[:, cs * 512:(cs + 1) * 512],
                                         in_=src_of(c), func=AF.Identity), "sS")
                inst = s.activation(out=XSQ[:, cs * 512:(cs + 1) * 512],
                                    in_=src_of(c), func=AF.Square)
                bk.inc(inst, "sS", 1)
                EV[f"{tag}_cast_{st}_{c}"] = bk.v("sS")
            SE(sc_)

            def tc_(t, c=c, cs=cs):
                bk.w(t, "sS", EV[f"{tag}_cast_{st}_{c}"])
                if c == 0:
                    bk.w(t, "sG", EV["p0_g"])
                    bk.w(t, "d_gw", EV["p0_gd"])
                    bk.w(t, "d0", EV["p0_sy"])
                t.matmul(ps_den[0:1, :], ONESB[:, 0:1], XBF[:, cs * 512:(cs + 1) * 512],
                         start=(c == 0), stop=(c == 7))
                inst = t.matmul(ps_m1[0:1, :], ONESB[:, 0:1], XSQ[:, cs * 512:(cs + 1) * 512],
                                start=(c == 0), stop=(c == 7))
                bk.inc(inst, "sT", 1)
                EV[f"{tag}_smm_{st}_{c}"] = bk.v("sT")
            TE(tc_)

        def post_v(v):
            bk.w(v, "sT", EV[f"{tag}_smm_{st}_7"])
            bk.inc(v.tensor_scalar(out=STATA[0:1, :], in0=ps_den[0:1, :], scalar1=1.0 / 1024.0,
                            scalar2=None, op0=OP.mult), "sV", 1)
            bk.chain(v, v.tensor_scalar(out=STATB[0:1, :], in0=ps_m1[0:1, :],
                                        scalar1=1.0 / 1024.0, scalar2=None, op0=OP.mult), "sV")
            bk.chain(v, v.tensor_tensor(out=SCR1[0:1, :], in0=STATA[0:1, :],
                                        in1=STATA[0:1, :], op=OP.mult), "sV")
            inst = v.tensor_tensor(out=STATB[0:1, :], in0=STATB[0:1, :],
                                   in1=SCR1[0:1, :], op=OP.subtract)
            bk.inc(inst, "sV", 1)
            EV[f"{tag}_var_{st}"] = bk.v("sV")
        VE(post_v)

        def post_s(s):
            bk.w(s, "sV", EV[f"{tag}_var_{st}"])
            bk.chain(s, s.activation(out=SCR1[0:1, :], in_=STATB[0:1, :], func=AF.Ln,
                                      bias=CST[0:1, 4:5]), "sS")
            inst = s.activation(out=STATB[0:1, :], in_=SCR1[0:1, :], func=AF.Exp, scale=-0.5)
            bk.inc(inst, "sS", 1)
            EV[f"{tag}_rstd_{st}"] = bk.v("sS")
        SE(post_s)

        def post_v3(v):
            bk.w(v, "sS", EV[f"{tag}_rstd_{st}"])
            bk.w(v, "sV")
            inst = v.tensor_tensor(out=STATA[0:1, :], in0=STATA[0:1, :],
                                   in1=STATB[0:1, :], op=OP.mult)
            bk.inc(inst, "sV", 1)
            EV[f"{tag}_brow_{st}"] = bk.v("sV")
        VE(post_v3)

        def rep_t(t):
            bk.w(t, "sV", EV[f"{tag}_brow_{st}"])
            t.matmul(ps_sc0[:, 0:512], ONES1R[:, :], STATB[0:1, :], start=True, stop=True)
            inst = t.matmul(ps_sc0[:, 512:1024], ONES1R[:, :], STATA[0:1, :],
                            start=True, stop=True)
            bk.inc(inst, "sT", 1)
            EV[f"{tag}_rep_{st}"] = bk.v("sT")
        TE(rep_t)


    for st in range(8):
        slot = st % 2

        def p1_load(sy, st=st, slot=slot):
            if st >= 2:
                bk.w(sy, "sV", EV[f"p1_qkv_{st-2}"])
            sem = f"dx{slot}"
            bk.inc(sy.dma_start(out=sb3(X0FR[:, slot * 4096:(slot + 1) * 4096], 512),
                                in_=fm3(x0f[:, st * 512:(st + 1) * 512])), sem, 16)
            EV[f"p1_x0_{st}"] = (sem, bk.v(sem))
        SY(p1_load)

        def xsl(c, slot=slot):
            return X0FR[:, slot * 4096 + c * 512: slot * 4096 + (c + 1) * 512]

        ln_stats("p1", st, xsl, EV[f"p1_x0_{st}"])

        def p1_norm(v, st=st, slot=slot):
            bk.w(v, "sT", EV[f"p1_rep_{st}"])
            for c in range(4):
                bk.w(v, "sV")
                bk.chain(v, v.tensor_tensor(out=SCR1[:, :], in0=xsl(c),
                                            in1=ps_sc0[:, 0:512], op=OP.mult), "sV")
                bk.inc(v.tensor_tensor(out=LN0[:, c * 4096 + st * 512: c * 4096 + (st + 1) * 512],
                                        in0=SCR1[:, :], in1=ps_sc0[:, 512:1024],
                                        op=OP.subtract), "sV", 1)
            EV[f"p1_ln0_{st}"] = bk.v("sV")
        VE(p1_norm)

        for hp in range(4):
            def tq(t, st=st, hp=hp):
                bk.w(t, "sV", EV[f"p1_ln0_{st}"])
                if st == 0 and hp == 0:
                    bk.w(t, "d_gw", EV["p0_gd"])
                if "p1_qk_ev" in EV:
                    bk.w(t, "sV", EV["p1_qk_ev"])
                t.matmul(ps_sc1[0:64, 0:512], WQ0[0:64, hp * 64:hp * 64 + 64],
                         LN0[0:64, hp * 4096 + st * 512: hp * 4096 + (st + 1) * 512],
                         start=True, stop=True, tile_position=(0, 0), skip_group_check=True)
                t.matmul(ps_sc1[64:128, 0:512], WQ0[64:128, hp * 64:hp * 64 + 64],
                         LN0[64:128, hp * 4096 + st * 512: hp * 4096 + (st + 1) * 512],
                         start=True, stop=True, tile_position=(64, 64), skip_group_check=True)
                t.matmul(ps_sc1[0:64, 512:1024], WK0[0:64, hp * 64:hp * 64 + 64],
                         LN0[0:64, hp * 4096 + st * 512: hp * 4096 + (st + 1) * 512],
                         start=True, stop=True, tile_position=(0, 0), skip_group_check=True)
                inst = t.matmul(ps_sc1[64:128, 512:1024], WK0[64:128, hp * 64:hp * 64 + 64],
                                LN0[64:128, hp * 4096 + st * 512: hp * 4096 + (st + 1) * 512],
                                start=True, stop=True, tile_position=(64, 64),
                                skip_group_check=True)
                bk.inc(inst, "sT", 1)
                EV["p1_qk_mm"] = bk.v("sT")
            TE(tq)

            def vq(v, st=st, hp=hp):
                bk.w(v, "sT", EV["p1_qk_mm"])
                bk.inc(v.tensor_copy(out=QFM[:, hp * 4096 + st * 512: hp * 4096 + (st + 1) * 512],
                              in_=ps_sc1[:, 0:512]), "sV", 1)
                inst = v.tensor_copy(out=KFM[:, hp * 4096 + st * 512: hp * 4096 + (st + 1) * 512],
                                     in_=ps_sc1[:, 512:1024])
                bk.inc(inst, "sV", 1)
                EV["p1_qk_ev"] = bk.v("sV")
            VE(vq)

            def tv(t, st=st, hp=hp):
                if "p1_v_ev" in EV:
                    bk.w(t, "sV", EV["p1_v_ev"])
                bk.w(t, "sV", EV[f"p1_ln0_{st}"])
                for ts4 in range(4):
                    gts = st * 4 + ts4
                    inst = t.matmul(ps_av[:, 0:512][:, ts4 * 128:(ts4 + 1) * 128],
                                    LN0[:, hp * 4096 + gts * 128: hp * 4096 + (gts + 1) * 128],
                                    WVD0[:, hp * 128:(hp + 1) * 128],
                                    start=True, stop=True)
                bk.inc(inst, "sT", 1)
                EV["p1_v_mm"] = bk.v("sT")
            TE(tv)

            def vv(v, st=st, hp=hp):
                bk.w(v, "sT", EV["p1_v_mm"])
                for ts4 in range(4):
                    gts = st * 4 + ts4
                    bk.inc(v.tensor_copy(
                        out=vout(gts * 8 + hp * 2, 2),
                        in_=ps_av[:, ts4 * 128:(ts4 + 1) * 128].rearrange(
                            "p (c t) -> p c t", t=64)), "sV", 1)
                EV["p1_v_ev"] = bk.v("sV")
            VE(vv)
        EV[f"p1_qkv_{st}"] = bk.v("sV")
    EV["p1_done"] = bk.v("sV")

    def norms_flat(tag, qbuf, kbuf, ev_ready):
        for hp in range(4):
            for st in range(8):
                def vn(v, hp=hp, st=st):
                    bk.w(v, ev_ready[0], ev_ready[1])
                    if "nrm_mmk" in EV:
                        bk.w(v, "sT", EV["nrm_mmk"])
                    if hp == 0 and st == 0:
                        bk.inc(v.memset(ps_m1[0:34, :], 0.0), "sV", 1)
                    inst = v.tensor_tensor(
                        out=QSQ[:, :],
                        in0=qbuf[:, hp * 4096 + st * 512: hp * 4096 + (st + 1) * 512],
                        in1=qbuf[:, hp * 4096 + st * 512: hp * 4096 + (st + 1) * 512],
                        op=OP.mult)
                    bk.inc(inst, "sV", 1)
                    EV["nrm_sq"] = bk.v("sV")
                VE(vn)

                def tn(t, hp=hp, st=st):
                    bk.w(t, "sV", EV["nrm_sq"])
                    inst = t.matmul(ps_m1[0:2, :], SELH[:, 0:2], QSQ[:, :],
                                    start=True, stop=True, tile_position=(0, 0),
                                    skip_group_check=True)
                    bk.inc(inst, "sT", 1)
                    EV["nrm_mmq"] = bk.v("sT")
                TE(tn)

                def vn2(v, hp=hp, st=st):
                    bk.w(v, "sT", EV["nrm_mmq"])
                    inst = v.tensor_tensor(
                        out=QSQ[:, :],
                        in0=kbuf[:, hp * 4096 + st * 512: hp * 4096 + (st + 1) * 512],
                        in1=kbuf[:, hp * 4096 + st * 512: hp * 4096 + (st + 1) * 512],
                        op=OP.mult)
                    bk.inc(inst, "sV", 1)
                    EV["nrm_sqk"] = bk.v("sV")
                VE(vn2)

                def tn2(t, hp=hp, st=st):
                    bk.w(t, "sV", EV["nrm_sqk"])
                    inst = t.matmul(ps_m1[32:34, :], SELH[:, 0:2], QSQ[:, :],
                                    start=True, stop=True, tile_position=(0, 32),
                                    skip_group_check=True)
                    bk.inc(inst, "sT", 1)
                    EV["nrm_mmk"] = bk.v("sT")
                TE(tn2)

                def vn3(v, hp=hp, st=st):
                    bk.w(v, "sT", EV["nrm_mmk"])
                    inst = v.reduce_max(out=NRMC[0:34, st:st + 1], in_=ps_m1[0:34, :], axis=AX.X)
                    bk.inc(inst, "sV", 1)
                    EV["nrm_red"] = bk.v("sV")
                VE(vn3)

            def vfin(v, hp=hp):
                bk.w(v, "sV")
                inst = v.reduce_max(out=NRMC[0:34, 8:9], in_=NRMC[0:34, 0:8], axis=AX.X)
                bk.inc(inst, "sV", 1)
                EV["nrm_fin"] = bk.v("sV")
            VE(vfin)

            def tfin(t, hp=hp):
                bk.w(t, "sV", EV["nrm_fin"])
                inst = t.transpose(ps_m1[0:1, 64:98], NRMC[0:34, 8:9], IDF[0:34, 0:34])
                bk.inc(inst, "sT", 1)
                EV["nrm_tr"] = bk.v("sT")
            TE(tfin)

            def vfin2(v, hp=hp, tag=tag):
                bk.w(v, "sT", EV["nrm_tr"])
                if hp > 0:
                    bk.w(v, "sG", EV[f"biasm_{tag}_{hp-1}"])
                bk.chain(v, v.tensor_copy(out=NRT[0:1, 0:34], in_=ps_m1[0:1, 64:98]), "sV")
                bk.chain(v, v.tensor_tensor(out=NRT[0:1, 40:42], in0=NRT[0:1, 0:2],
                                            in1=NRT[0:1, 32:34], op=OP.add), "sV")
                bk.chain(v, v.reduce_max(out=NRT[0:1, 48:49], in_=NRT[0:1, 40:42], axis=AX.X,
                                         negate=True), "sV")
                inst = v.tensor_scalar(out=NRT[0:1, 49:50], in0=NRT[0:1, 48:49],
                                       scalar1=1.0 / 16.0, scalar2=None, op0=OP.mult)
                bk.inc(inst, "sV", 1)
                EV["nrm_m"] = bk.v("sV")
            VE(vfin2)

            def gfin(g, hp=hp, tag=tag):
                bk.w(g, "sV", EV["nrm_m"])
                inst = g.partition_broadcast(BIASM[:, hp:hp + 1], NRT[0:1, 49:50])
                bk.inc(inst, "sG", 1)
                EV[f"biasm_{tag}_{hp}"] = bk.v("sG")
            GE(gfin)

    if DBG:
        dump(dbg_ln0, LN0[:, :])
        dump(dbg_qfm, QFM[:, :])
        dump(dbg_kfm, KFM[:, :])
        dump(dbg_vtm, VTM[:, :])
    norms_flat("l0", QFM, KFM, ("sV", EV["p1_done"]))

    for which in range(2):
        usrc = uRd if which == 0 else uId
        vsrc = vRd if which == 0 else vId
        for c in range(4):
            for ch in range(4):
                cs = ch % 2

                def ld(sy, c=c, ch=ch, cs=cs, usrc=usrc, vsrc=vsrc):
                    if f"r12_mul_m2" in EV:
                        bk.w(sy, "sV", EV["r12_mul_m2"])
                    sem = f"d_uv{cs}"
                    bk.inc(sy.dma_start(out=UVU[:, cs * 512:(cs + 1) * 512],
                                        in_=usrc[c][ch * 128:(ch + 1) * 128, :]), sem, 16)
                    bk.inc(sy.dma_start(out=UVV[:, cs * 512:(cs + 1) * 512],
                                        in_=vsrc[c][ch * 128:(ch + 1) * 128, :]), sem, 16)
                    EV["r12_ld"] = (sem, bk.v(sem))
                SY(ld)

                def mul(v, c=c, ch=ch, cs=cs):
                    bk.w(v, EV["r12_ld"][0], EV["r12_ld"][1])
                    if c == 0:
                        inst = v.tensor_tensor(out=WTACC[:, ch * 512:(ch + 1) * 512],
                                               in0=UVU[:, cs * 512:(cs + 1) * 512],
                                               in1=UVV[:, cs * 512:(cs + 1) * 512], op=OP.mult)
                    else:
                        bk.chain(v, v.tensor_tensor(out=SCR1[:, :],
                                                     in0=UVU[:, cs * 512:(cs + 1) * 512],
                                                     in1=UVV[:, cs * 512:(cs + 1) * 512],
                                                     op=OP.mult), "sV")
                        inst = v.tensor_tensor(out=WTACC[:, ch * 512:(ch + 1) * 512],
                                               in0=WTACC[:, ch * 512:(ch + 1) * 512],
                                               in1=SCR1[:, :], op=OP.add)
                    nv_ = bk.inc(inst, "sV", 1)
                    EV["r12_mul"] = nv_
                    EV["r12_mul_m2"] = EV.get("r12_mul_prev", 0)
                    EV["r12_mul_prev"] = nv_
                VE(mul)

        def cast(v, which=which):
            bk.w(v, "sV")
            for ch in range(4):
                src = WTACC[:, ch * 512:(ch + 1) * 512]
                if which == 0:
                    bk.inc(v.tensor_copy(out=R12B[:, ch * 512:(ch + 1) * 512], in_=src), "sV", 1)
                    bk.inc(v.tensor_copy(
                        out=R12B[:, 4096 + (4 + ch) * 512: 4096 + (5 + ch) * 512], in_=src),
                        "sV", 1)
                else:
                    bk.inc(v.tensor_copy(out=R12B[:, 4096 + ch * 512: 4096 + (ch + 1) * 512], in_=src), "sV", 1)
                    bk.inc(v.tensor_scalar(out=R12B[:, (4 + ch) * 512:(5 + ch) * 512],
                                           in0=src, scalar1=-1.0, scalar2=None, op0=OP.mult),
                           "sV", 1)
            EV["r12_cast"] = bk.v("sV")
        VE(cast)

    def r12_spill(sy):
        bk.w(sy, "sV", EV["r12_cast"])
        r3 = r12sp.ap().rearrange("r c p t -> p r c t")
        r12bv = R12B[:, :].rearrange("p (r c t) -> p r c t", r=2, t=512)
        bk.inc(sy.dma_start(out=r3, in_=r12bv), "d_rsp", 16)
        EV["r12_sp"] = bk.v("d_rsp")
    SY(r12_spill)

    def mbig_load(sy):
        bk.w(sy, "d_rsp", EV["r12_sp"])
        bk.inc(sy.dma_start(out=MBIG[:, :], in_=mbig), "d_mb", 16)
        EV["mbig0"] = bk.v("d_mb")
    SY(mbig_load)

    def attention(layer, dst_buf, rrep_mode, qs_end=None):
        tagm = "l0" if layer == 0 else "l1"

        exp_ev = {}
        avd_ev = {}
        sc_ev = {}
        msk_ev = {}

        def avd_mm(t, hp, kbd, gd, nkb):
            bk.w(t, "sS", exp_ev[gd])
            if gd in msk_ev:
                bk.w(t, "sV", msk_ev[gd])
            if kbd == 0:
                if "att_evac" in EV:
                    bk.w(t, "sV", EV["att_evac"])
                if "att_lnd" in EV:
                    bk.w(t, "sS", EV["att_lnd"])
                if f"vones{layer}" in EV:
                    bk.w(t, "sV", EV[f"vones{layer}"])
            ps = gd % 8
            t.matmul(ps_av[0:65, :], vsl(kbd, hp, 0),
                     PT[:, ps * 1024: ps * 1024 + 512],
                     start=(kbd == 0), stop=(kbd == nkb - 1))
            inst = t.matmul(ps_den[0:65, :], vsl(kbd, hp, 1),
                            PT[:, ps * 1024 + 512:(ps + 1) * 1024],
                            start=(kbd == 0), stop=(kbd == nkb - 1))
            bk.inc(inst, "sT", 1)
            avd_ev[gd] = bk.v("sT")

        for hp in range(4):
            gkb = 0
            for qs in range(8):
                nkb = 4 * qs + 4

                def emit_scores(k, t, hp=hp, qs=qs):
                    kb = k
                    slot = gkb_of[k] % 2
                    scb = ps_sc0 if slot == 0 else ps_sc1
                    if gkb_of[k] - 2 in gexp:
                        bk.w(t, "sS", gexp[gkb_of[k] - 2])
                    t.matmul(scb[:, 0:512],
                             KFM[0:64, hp * 4096 + kb * 128: hp * 4096 + (kb + 1) * 128],
                             QFM[0:64, hp * 4096 + qs * 512: hp * 4096 + (qs + 1) * 512],
                             start=True, stop=True, tile_position=(0, 0))
                    inst = t.matmul(scb[:, 512:1024],
                                    KFM[64:128, hp * 4096 + kb * 128: hp * 4096 + (kb + 1) * 128],
                                    QFM[64:128, hp * 4096 + qs * 512: hp * 4096 + (qs + 1) * 512],
                                    start=True, stop=True, tile_position=(64, 0))
                    bk.inc(inst, "sT", 1)
                    gsc[gkb_of[k]] = bk.v("sT")

                pass

                def mk_scores(kb, g, scb, hp=hp, qs=qs):
                    def t_scores(t, hp=hp, qs=qs, kb=kb, g=g, scb=scb):
                        if g == 0 and hp == 0:
                            bk.w(t, "sG", EV[f"biasm_{tagm}_3"])
                            bk.w(t, "d_mb" if layer == 0 else "d_mb2",
                                 EV["mbig0" if layer == 0 else "mbig1"])
                        if g - 2 in exp_ev:
                            bk.w(t, "sS", exp_ev[g - 2])
                        t.matmul(scb[:, 0:512],
                                 KFM[0:64, hp * 4096 + kb * 128: hp * 4096 + (kb + 1) * 128],
                                 QFM[0:64, hp * 4096 + qs * 512: hp * 4096 + (qs + 1) * 512],
                                 start=True, stop=True, tile_position=(0, 0))
                        inst = t.matmul(scb[:, 512:1024],
                                        KFM[64:128, hp * 4096 + kb * 128: hp * 4096 + (kb + 1) * 128],
                                        QFM[64:128, hp * 4096 + qs * 512: hp * 4096 + (qs + 1) * 512],
                                        start=True, stop=True, tile_position=(64, 0))
                        bk.inc(inst, "sT", 1)
                        sc_ev[g] = bk.v("sT")
                    return t_scores

                for kb in range(nkb):
                    g = gkb + kb
                    diag = kb >= 4 * qs
                    j = kb - 4 * qs
                    slot = g % 2
                    scb = ps_sc0 if slot == 0 else ps_sc1
                    pslot = g % 8

                    if kb % 2 == 0:
                        TE(mk_scores(kb, g, scb))
                        TE(mk_scores(kb + 1, g + 1,
                                     ps_sc0 if (g + 1) % 2 == 0 else ps_sc1))

                    def s_exp(s, hp=hp, g=g, scb=scb, pslot=pslot):
                        bk.w(s, "sT", sc_ev[g])
                        if g - 8 in avd_ev:
                            bk.w(s, "sT", avd_ev[g - 8])
                        inst = s.activation(out=PT[:, pslot * 1024:(pslot + 1) * 1024],
                                            in_=scb[:, :], func=AF.Exp,
                                            bias=BIASM[:, hp:hp + 1], scale=0.125)
                        bk.inc(inst, "sS", 1)
                        exp_ev[g] = bk.v("sS")
                    SE(s_exp)

                    if diag:
                        def v_mask(v, g=g, j=j, pslot=pslot):
                            bk.w(v, "sS", exp_ev[g])
                            inst = v.tensor_tensor(
                                out=PT[:, pslot * 1024:(pslot + 1) * 1024],
                                in0=PT[:, pslot * 1024:(pslot + 1) * 1024],
                                in1=MBIG[:, j * 1024:(j + 1) * 1024], op=OP.mult)
                            bk.inc(inst, "sV", 1)
                            msk_ev[g] = bk.v("sV")
                        VE(v_mask)

                    gd = g - 2
                    if gd >= gkb:
                        TE(lambda t, hp=hp, kbd=gd - gkb, gd=gd, nkb=nkb:
                           avd_mm(t, hp, kbd, gd, nkb))

                for gd in (gkb + nkb - 2, gkb + nkb - 1):
                    TE(lambda t, hp=hp, kbd=gd - gkb, gd=gd, nkb=nkb:
                       avd_mm(t, hp, kbd, gd, nkb))
                EV["att_av"] = bk.v("sT")

                if rrep_mode:
                    def s_den(s, hp=hp, qs=qs):
                        bk.w(s, "sT", EV["att_av"])
                        bk.chain(s, s.activation(out=S1D[64:65, :], in_=ps_avd[64:65, :],
                                                 func=AF.Ln), "sS")
                        inst = s.activation(out=RDD[64:65, 0:1024], in_=S1D[64:65, :],
                                            func=AF.Exp, scale=-1.0)
                        bk.inc(inst, "sS", 1)
                        EV["att_lnd"] = bk.v("sS")
                    SE(s_den)

                    def t_rep(t, hp=hp, qs=qs):
                        bk.w(t, "sS", EV["att_lnd"])
                        t.matmul(ps_m1[0:64, 0:512], ONESR[64:65, 0:64],
                                 RDD[64:65, 0:512], start=True, stop=True,
                                 tile_position=(64, 0), skip_group_check=True)
                        inst = t.matmul(ps_m1[64:128, 0:512], ONESR[64:65, 0:64],
                                        RDD[64:65, 512:1024], start=True, stop=True,
                                        tile_position=(64, 64), skip_group_check=True)
                        bk.inc(inst, "sT", 1)
                        EV["att_rep"] = bk.v("sT")
                    TE(t_rep)

                    def v_evac(v, hp=hp, qs=qs):
                        bk.w(v, "sT", EV["att_rep"])
                        bk.chain(v, v.tensor_copy(out=RREPS[:, :], in_=ps_m1[:, 0:512]), "sV")
                        cs = hp * 4096 + qs * 512
                        bk.inc(v.tensor_tensor(
                            out=dst_buf[0:64, cs:cs + 512],
                            in0=ps_av[0:64, :], in1=RREPS[0:64, :], op=OP.mult), "sV", 1)
                        inst = v.tensor_tensor(
                            out=dst_buf[64:128, cs:cs + 512],
                            in0=ps_den[0:64, :], in1=RREPS[64:128, :], op=OP.mult)
                        bk.inc(inst, "sV", 1)
                        EV["att_evac"] = bk.v("sV")
                    VE(v_evac)
                else:
                    def v_den(v, hp=hp, qs=qs):
                        bk.w(v, "sT", EV["att_av"])
                        inst = v.tensor_copy(out=RDD[64:65, 0:1024],
                                             in_=ps_avd[64:65, :])
                        bk.inc(inst, "sV", 1)
                        EV["att_dencp"] = bk.v("sV")
                    VE(v_den)

                    def t_rdt(t, hp=hp, qs=qs):
                        bk.w(t, "sV", EV["att_dencp"])
                        for qb in range(4):
                            t.transpose(ps_m1[:, qb:qb + 1],
                                        RDD[64:65, qb * 128:(qb + 1) * 128],
                                        ONESF[64:65, 0:1])
                        for qb in range(4):
                            inst = t.transpose(ps_m1[:, 4 + qb:5 + qb],
                                               RDD[64:65, 512 + qb * 128: 512 + (qb + 1) * 128],
                                               ONESF[64:65, 0:1])
                        bk.inc(inst, "sT", 1)
                        EV["att_rdt"] = bk.v("sT")
                    TE(t_rdt)

                    def v_evac(v, hp=hp, qs=qs):
                        bk.w(v, "sT", EV["att_rdt"])
                        bk.inc(v.tensor_copy(out=RDT[:, hp * 64 + qs * 8: hp * 64 + qs * 8 + 8],
                                             in_=ps_m1[:, 0:8]), "sV", 1)
                        cs = hp * 4096 + qs * 512
                        bk.inc(v.tensor_copy(out=dst_buf[0:64, cs:cs + 512],
                                             in_=ps_av[0:64, :]), "sV", 1)
                        inst = v.tensor_copy(out=dst_buf[64:128, cs:cs + 512],
                                             in_=ps_den[0:64, :])
                        bk.inc(inst, "sV", 1)
                        EV["att_evac"] = bk.v("sV")
                    VE(v_evac)
                gkb += nkb
                if qs_end is not None:
                    qs_end(hp, qs)
        EV[f"attn{layer}_done"] = bk.v("sV")

    def scr_rcp(c):
        def s_rcp(s, c=c):
            bk.w(s, "sV", EV["att_evac"])
            bk.chain(s, s.activation(out=RDTL[:, c * 64:(c + 1) * 64],
                                     in_=RDT[:, c * 64:(c + 1) * 64], func=AF.Ln), "sS")
            inst = s.activation(out=RDT[:, c * 64:(c + 1) * 64],
                                in_=RDTL[:, c * 64:(c + 1) * 64], func=AF.Exp, scale=-1.0)
            bk.inc(inst, "sS", 1)
            EV[f"rdt_rcp_{c}"] = bk.v("sS")
        SE(s_rcp)

    def scr_group(c, g):
        def tt_(t, c=c, g=g):
            if g == 0:
                bk.w(t, "sV", EV[f"scr_cat_{c}"])
            if "scr_ev" in EV:
                bk.w(t, "sV", EV["scr_ev"])
            for wi in range(4):
                w = g * 4 + wi
                inst = t.transpose(ps_tb[:, wi * 128:(wi + 1) * 128],
                                   CAT[:, c * 4096 + w * 128: c * 4096 + (w + 1) * 128],
                                   IDB[:, :])
            bk.inc(inst, "sT", 1)
            EV["scr_tr"] = bk.v("sT")
        TE(tt_)

        def vv_(v, c=c, g=g):
            bk.w(v, "sT", EV["scr_tr"])
            bk.w(v, "sS", EV[f"rdt_rcp_{c}"])
            for wi in range(4):
                w = g * 4 + wi
                ec = w % 8
                sq = w // 8
                base = c * 4096 + ec * 512 + sq
                rcol = c * 64 + (w // 4) * 8
                bk.inc(v.tensor_scalar(out=SCRM[:, base: base + 253: 4],
                                       in0=ps_tb[:, wi * 128:wi * 128 + 64],
                                       scalar1=RDT[:, rcol + wi:rcol + wi + 1],
                                       scalar2=None, op0=OP.mult), "sV", 1)
                bk.inc(v.tensor_scalar(out=SCRM[:, base + 256: base + 509: 4],
                                       in0=ps_tb[:, wi * 128 + 64:(wi + 1) * 128],
                                       scalar1=RDT[:, rcol + 4 + wi:rcol + 5 + wi],
                                       scalar2=None, op0=OP.mult), "sV", 1)
            EV["scr_ev"] = bk.v("sV")
        VE(vv_)

    def scr_hook(hp, qs):
        if qs == 7:
            EV[f"scr_cat_{hp}"] = EV["att_evac"]
            scr_rcp(hp)
        if hp >= 1:
            scr_group(hp - 1, qs)

    attention(0, CAT, rrep_mode=False, qs_end=scr_hook)
    for g in range(8):
        scr_group(3, g)
    EV["p3_scr"] = bk.v("sV")

    if DBG:
        dump(dbg_cat, CAT[:, :])
        dump(dbg_rdt, RDT[:, :])

    def p3_wload(g):
        bk.w(g, "sV", EV["attn0_done"])
        inst = g.iota(ONESI[:, :], pattern=[[0, 512]], base=1, channel_multiplier=0)
        bk.inc(inst, "sG", 1)
        EV["onesi"] = bk.v("sG")
        bk.inc(g.dma_start(out=sb3(WOB[:, :], 1024), in_=fm3(wo0)), "d_wo", 16)
        EV["p3_wo"] = bk.v("d_wo")
    GE(p3_wload)

    def p3_wfload(sy):
        bk.w(sy, "sV", EV["attn0_done"])
        bk.inc(sy.dma_start(out=sb3(WF32[:, :], 1024), in_=fm3(wf0)), "d_wf", 16)
        EV["p3_wf"] = bk.v("d_wf")
    SY(p3_wfload)


    if DBG:
        dump(dbg_scrm, SCRM[:, :])

    def wam_v1(v):
        bk.w(v, "d_wf", EV["p3_wf"])
        for c in range(7):
            bk.inc(v.reduce_max(out=AMXC[:, c:c + 1], in_=WF32[:, c * 1024:(c + 1) * 1024],
                         axis=AX.X, apply_absolute_value=True), "sV", 1)
        bk.chain(v, v.reduce_max(out=AMXC[:, 7:8], in_=WF32[:, 7 * 1024:8 * 1024],
                                 axis=AX.X, apply_absolute_value=True), "sV")
        inst = v.reduce_max(out=AMXC[:, 8:9], in_=AMXC[:, 0:8], axis=AX.X)
        bk.inc(inst, "sV", 1)
        EV["wam_v"] = bk.v("sV")
    VE(wam_v1)

    def wam_t1(t):
        bk.w(t, "sV", EV["wam_v"])
        inst = t.transpose(ps_m1[0:1, 128:256], AMXC[:, 8:9], IDF[:, :])
        bk.inc(inst, "sT", 1)
        EV["wam_t"] = bk.v("sT")
    TE(wam_t1)

    def wam_v2(v):
        bk.w(v, "sT", EV["wam_t"])
        bk.chain(v, v.reduce_max(out=SC1[0:1, 0:1], in_=ps_m1[0:1, 128:256], axis=AX.X), "sV")
        bk.chain(v, v.tensor_scalar(out=SC1[0:1, 1:2], in0=SC1[0:1, 0:1], scalar1=1e-12,
                                    scalar2=None, op0=OP.max), "sV")
        bk.chain(v, v.reciprocal(out=SC1[0:1, 2:3], in_=SC1[0:1, 1:2]), "sV")
        inst = v.tensor_scalar(out=SC1[0:1, 3:4], in0=SC1[0:1, 2:3], scalar1=448.0,
                               scalar2=None, op0=OP.mult)
        bk.inc(inst, "sV", 1)
        EV["wam_s"] = bk.v("sV")
    VE(wam_v2)

    def wam_g1(g):
        bk.w(g, "sV", EV["wam_s"])
        inst = g.partition_broadcast(SCWV[:, :], SC1[0:1, 3:4])
        bk.inc(inst, "sG", 1)
        EV["wam_b"] = bk.v("sG")
    GE(wam_g1)

    def quant_tile(src, dst, sv, waits, alt=False):
        sid = 1 if alt else 0
        qa, qz, qe, qsg = (QA1, QZ1, QE1, QSG1) if alt else (QA, QZ, QE, QSG)

        def s1(s):
            for wn, wv_ in waits:
                bk.w(s, wn, wv_)
            if f"q_v4_{sid}" in EV:
                bk.w(s, "sV", EV[f"q_v4_{sid}"])
            bk.chain(s, s.activation(out=qa[:, :], in_=src, func=AF.Abs,
                                     scale=sv[:, 0:1]), "sS")
            bk.chain(s, s.activation(out=qz[:, :], in_=qa[:, :], func=AF.Ln,
                                     bias=CST[:, 0:1]), "sS")
            bk.inc(s.activation(out=qe[:, :], in_=qz[:, :], func=AF.Identity,
                                scale=INVLN2, bias=CST[:, 1:2]), "sS", 1)
            bk.inc(s.activation(out=qsg[:, :], in_=src, func=AF.Sign), "sS", 1)
            EV["q_s1"] = bk.v("sS")
        SE(s1)

        def v1(v):
            bk.w(v, "sS", EV["q_s1"])
            bk.chain(v, v.scalar_tensor_tensor(out=qz[:, :], in0=qz[:, :], scalar=INVLN2,
                                               in1=qe[:, :], op0=OP.mult, op1=OP.subtract), "sV")
            bk.chain(v, v.tensor_scalar(out=QTC[:, :], in0=qz[:, :], scalar1=-16.0,
                                        scalar2=None, op0=OP.is_lt), "sV")
            bk.chain(v, v.tensor_tensor(out=qe[:, :], in0=qe[:, :], in1=QTC[:, :],
                                        op=OP.subtract), "sV")
            bk.chain(v, v.tensor_scalar(out=QT[:, :], in0=qe[:, :], scalar1=-10.0,
                                        scalar2=0.0, op0=OP.add, op1=OP.max), "sV")
            bk.chain(v, v.tensor_scalar(out=qe[:, :], in0=QT[:, :], scalar1=-1.0,
                                        scalar2=15.0, op0=OP.mult, op1=OP.add), "sV")
            bk.chain(v, v.tensor_tensor(out=QM[:, :], in0=ONESI[:, :], in1=QT[:, :],
                                        op=OP.logical_shift_left), "sV")
            bk.chain(v, v.tensor_tensor(out=QT[:, :], in0=ONESI[:, :], in1=qe[:, :],
                                        op=OP.logical_shift_left), "sV")
            bk.chain(v, v.scalar_tensor_tensor(out=qz[:, :], in0=qa[:, :],
                                               scalar=float(2.0 ** -6), in1=QT[:, :],
                                               op0=OP.mult, op1=OP.mult), "sV")
            bk.chain(v, v.tensor_scalar(out=qe[:, :], in0=qz[:, :], scalar1=1.0,
                                        scalar2=None, op0=OP.mult), "sV")
            bk.chain(v, v.scalar_tensor_tensor(out=QTC[:, :], in0=qz[:, :], scalar=-0.5,
                                               in1=qe[:, :], op0=OP.add, op1=OP.is_gt), "sV")
            bk.chain(v, v.tensor_tensor(out=qe[:, :], in0=qe[:, :], in1=QTC[:, :],
                                        op=OP.add), "sV")
            bk.chain(v, v.scalar_tensor_tensor(out=qz[:, :], in0=qe[:, :],
                                               scalar=float(2.0 ** -9), in1=QM[:, :],
                                               op0=OP.mult, op1=OP.mult), "sV")
            bk.inc(v.tensor_tensor(out=dst, in0=qz[:, :], in1=qsg[:, :], op=OP.mult),
                   "sV", 1)
            EV[f"q_v4_{sid}"] = bk.v("sV")
        VE(v1)
        return bk.v("sV")

    for ti in range(16):
        quant_tile(WF32[:, ti * 512:(ti + 1) * 512], WQT[:, ti * 512:(ti + 1) * 512],
                   SCWV, [("sG", EV["wam_b"]), ("sG", EV["onesi"])])
    EV["p3_wq"] = bk.v("sV")

    for st in range(4):
        slot = st % 2

        def p4_x0o(sy, st=st, slot=slot):
            if st < 2:
                bk.w(sy, "sV", EV["p3_scr"])
            else:
                bk.w(sy, "sV", EV[f"p4_x1_{st-2}"])
            sem = f"dx{slot}"
            bk.inc(sy.dma_start(out=sb3(X0OR[:, slot * 4096:(slot + 1) * 4096], 512),
                                in_=fm3(x0o[:, st * 512:(st + 1) * 512])), sem, 16)
            EV[f"p4_x0o_{st}"] = (sem, bk.v(sem))
        SY(p4_x0o)

        for oc in range(8):
            half = (st * 8 + oc) % 2
            pb = ps_sc0 if half == 0 else ps_sc1

            def p4_mm(t, st=st, oc=oc, pb=pb, half=half):
                if st == 0 and oc == 0:
                    bk.w(t, "sV", EV["p3_scr"])
                    bk.w(t, "d_wo", EV["p3_wo"])
                if f"p4_ev_h{half}" in EV:
                    bk.w(t, "sV", EV[f"p4_ev_h{half}"])
                for ec in range(8):
                    inst = t.matmul(pb[:, 0:512],
                                    WOB[:, ec * 1024 + oc * 128: ec * 1024 + (oc + 1) * 128],
                                    SCRM[:, st * 4096 + ec * 512: st * 4096 + (ec + 1) * 512],
                                    start=(ec == 0), stop=(ec == 7))
                bk.inc(inst, "sT", 1)
                EV["p4_mm"] = bk.v("sT")
            TE(p4_mm)

            def p4_ev(v, st=st, oc=oc, slot=slot, pb=pb, half=half):
                bk.w(v, "sT", EV["p4_mm"])
                bk.w(v, EV[f"p4_x0o_{st}"][0], EV[f"p4_x0o_{st}"][1])
                if st == 0 and oc == 0:
                    bk.w(v, "d_rsp", EV["r12_sp"])
                inst = v.tensor_tensor(
                    out=STRM[:, oc * 2048 + st * 512: oc * 2048 + (st + 1) * 512],
                    in0=pb[:, 0:512],
                    in1=X0OR[:, slot * 4096 + oc * 512: slot * 4096 + (oc + 1) * 512],
                    op=OP.add)
                bk.inc(inst, "sV", 1)
                EV[f"p4_ev_h{half}"] = bk.v("sV")
            VE(p4_ev)
        EV[f"p4_x1_{st}"] = bk.v("sV")
    EV["p4_done"] = bk.v("sV")

    if DBG:
        dump(dbg_x1, STRM[:, :])
        dump(dbg_wqt, WQT[:, :])

    last_sp = {0: None, 1: None}
    for st in range(4):
        def ssrc(c, st=st):
            return STRM[:, c * 2048 + st * 512: c * 2048 + (st + 1) * 512]
        ln_stats("ln1", st, ssrc, ("sV", EV["p4_done"]))

        for c in range(8):
            cs = c % 2

            def nv(v, st=st, c=c, cs=cs):
                bk.w(v, "sT", EV[f"ln1_rep_{st}"])
                if last_sp[cs] is not None:
                    bk.w(v, last_sp[cs][0], last_sp[cs][1])
                bk.chain(v, v.tensor_tensor(out=SCR1[:, :], in0=ssrc(c),
                                             in1=ps_sc0[:, 0:512], op=OP.mult), "sV")
                bk.chain(v, v.tensor_tensor(out=LN1T[:, cs * 512:(cs + 1) * 512],
                                            in0=SCR1[:, :], in1=ps_sc0[:, 512:1024],
                                            op=OP.subtract), "sV")
                inst = v.reduce_max(out=AMXC[:, st * 8 + c: st * 8 + c + 1],
                                    in_=LN1T[:, cs * 512:(cs + 1) * 512], axis=AX.X,
                                    apply_absolute_value=True)
                bk.inc(inst, "sV", 1)
                EV[f"ln1_t_{st}_{c}"] = bk.v("sV")
            VE(nv)

            def nd(sy, st=st, c=c, cs=cs):
                bk.w(sy, "sV", EV[f"ln1_t_{st}_{c}"])
                sem = f"d_sp{cs}"
                bk.inc(sy.dma_start(out=ln1sp.ap()[c][:, st * 512:(st + 1) * 512],
                                    in_=LN1T[:, cs * 512:(cs + 1) * 512]), sem, 16)
                EV[f"ln1_dma_{st}_{c}"] = (sem, bk.v(sem))
                last_sp[cs] = (sem, bk.v(sem))
            SY(nd)
    EV["ln1_alldma"] = (bk.v("d_sp0"), bk.v("d_sp1"))
    EV["ln1_allv"] = bk.v("sV")

    def am_v1(v):
        bk.w(v, "sV", EV["ln1_allv"])
        inst = v.reduce_max(out=SCR1[:, 0:1], in_=AMXC[:, 0:32], axis=AX.X)
        bk.inc(inst, "sV", 1)
        EV["am_v"] = bk.v("sV")
    VE(am_v1)

    def am_t1(t):
        bk.w(t, "sV", EV["am_v"])
        inst = t.transpose(ps_m1[0:1, 128:256], SCR1[:, 0:1], IDF[:, :])
        bk.inc(inst, "sT", 1)
        EV["am_t"] = bk.v("sT")
    TE(am_t1)

    def am_v2(v):
        bk.w(v, "sT", EV["am_t"])
        bk.chain(v, v.memset(SCAL[0:1, :], 0.0), "sV")
        inst = v.reduce_max(out=SCAL[0:1, 0:1], in_=ps_m1[0:1, 128:256], axis=AX.X)
        bk.inc(inst, "sV", 1)
        EV["am_l"] = bk.v("sV")
    VE(am_v2)

    def am_g1(g):
        bk.w(g, "sV", EV["am_l"])
        bk.inc(g.dma_start(out=am_in.ap(), in_=SCAL[0:1, :]), "d_am", 16)
        bk.w(g, "d_am")
        inst = g.collective_compute("AllReduce", OP.max,
                                    replica_groups=[list(range(NCORES))],
                                    ins=[am_in.ap().opt()], outs=[am_out.ap().opt()])
        bk.inc(inst, "cc", 1)
        bk.w(g, "cc")
        bk.inc(g.dma_start(out=SCAL[0:1, :], in_=am_out.ap()), "d_am", 16)
        bk.w(g, "d_am")
        bk.inc(g.nop(), "sG", 1)
        EV["am_g"] = bk.v("sG")
    GE(am_g1)

    def am_v3(v):
        bk.w(v, "sG", EV["am_g"])
        bk.chain(v, v.tensor_scalar(out=SC1[0:1, 4:5], in0=SCAL[0:1, 0:1], scalar1=1e-12,
                                     scalar2=None, op0=OP.max), "sV")
        bk.chain(v, v.reciprocal(out=SC1[0:1, 5:6], in_=SC1[0:1, 4:5]), "sV")
        bk.chain(v, v.tensor_scalar(out=SC1[0:1, 4:5], in0=SC1[0:1, 5:6], scalar1=448.0,
                                    scalar2=None, op0=OP.mult), "sV")
        bk.chain(v, v.reciprocal(out=SC1[0:1, 6:7], in_=SC1[0:1, 4:5]), "sV")
        bk.chain(v, v.reciprocal(out=SC1[0:1, 7:8], in_=SC1[0:1, 3:4]), "sV")
        inst = v.tensor_tensor(out=SC1[0:1, 8:9], in0=SC1[0:1, 6:7], in1=SC1[0:1, 7:8],
                               op=OP.mult)
        bk.inc(inst, "sV", 1)
        EV["am_s"] = bk.v("sV")
    VE(am_v3)

    def am_g2(g):
        bk.w(g, "sV", EV["am_s"])
        bk.inc(g.partition_broadcast(SCXV[:, :], SC1[0:1, 4:5]), "sG", 1)
        inst = g.partition_broadcast(DSCV[:, :], SC1[0:1, 8:9])
        bk.inc(inst, "sG", 1)
        EV["am_b"] = bk.v("sG")
    GE(am_g2)

    if DBG:
        def _dsc(v):
            bk.w(v, "sG", EV["am_b"])
            bk.inc(v.tensor_copy(out=SCR1[:, 0:1], in_=SCXV[:, :]), "sV", 1)
            bk.inc(v.tensor_copy(out=SCR1[:, 1:2], in_=SCWV[:, :]), "sV", 1)
            bk.inc(v.tensor_copy(out=SCR1[:, 2:3], in_=DSCV[:, :]), "sV", 1)
            bk.inc(v.tensor_copy(out=SCR1[0:1, 3:4], in_=SCAL[0:1, 0:1]), "sV", 1)
        VE(_dsc)
        dump(dbg_sc, SCR1[:, 0:4])

    for st in range(4):
        slot = st % 2
        for c in range(8):
            cs = c % 2

            def p5_ld(sy, st=st, c=c, cs=cs):
                bk.w(sy, "d_sp0", EV["ln1_alldma"][0])
                bk.w(sy, "d_sp1", EV["ln1_alldma"][1])
                ti = st * 8 + c
                if f"p5_qt_{ti-2}" in EV:
                    bk.w(sy, "sV", EV[f"p5_qt_{ti-2}"])
                if f"p6_mm_{st-1}" in EV and c == 0:
                    bk.w(sy, "sT", EV[f"p6_mm_{st-1}"])
                sem = f"dx{cs}"
                bk.inc(sy.dma_start(out=LN1R[:, cs * 512:(cs + 1) * 512],
                                    in_=ln1sp.ap()[c][:, st * 512:(st + 1) * 512]), sem, 16)
                EV[f"p5_ld_{st}_{c}"] = (sem, bk.v(sem))
            SY(p5_ld)

            ti = st * 8 + c
            xw = [EV[f"p5_ld_{st}_{c}"], ("sG", EV["am_b"])]
            if ti % 2 == 1:
                xw.append(("sV", EV["p3_scr"]))
            EV[f"p5_qt_{ti}"] = quant_tile(
                LN1R[:, cs * 512:(cs + 1) * 512],
                XQR[:, slot * 4096 + c * 512: slot * 4096 + (c + 1) * 512],
                SCXV, xw, alt=(ti % 2 == 1))
        EV[f"p5_q_{st}"] = bk.v("sV")

        for oc in range(8):
            half = oc % 2
            pb = ps_sc0 if half == 0 else ps_sc1

            def p6_mm(t, st=st, oc=oc, slot=slot, pb=pb, half=half):
                bk.w(t, "sV", EV[f"p5_q_{st}"])
                if f"p6_ev_h{half}" in EV:
                    bk.w(t, "sV", EV[f"p6_ev_h{half}"])
                for ic in range(8):
                    inst = t.matmul(pb[:, 0:512],
                                    WQT[:, ic * 1024 + oc * 128: ic * 1024 + oc * 128 + 128],
                                    XQR[:, slot * 4096 + ic * 512: slot * 4096 + (ic + 1) * 512],
                                    start=(ic == 0), stop=(ic == 7))
                bk.inc(inst, "sT", 1)
                EV["p6_mmx"] = bk.v("sT")
            TE(p6_mm)

            def p6_ev(v, st=st, oc=oc, pb=pb, half=half):
                bk.w(v, "sT", EV["p6_mmx"])
                bk.chain(v, v.tensor_scalar(out=SCR1[:, :], in0=pb[:, 0:512],
                                             scalar1=DSCV[:, 0:1], scalar2=BF0[:, oc:oc + 1],
                                             op0=OP.mult, op1=OP.add), "sV")
                inst = v.tensor_tensor(
                    out=STRM[:, oc * 2048 + st * 512: oc * 2048 + (st + 1) * 512],
                    in0=SCR1[:, :],
                    in1=STRM[:, oc * 2048 + st * 512: oc * 2048 + (st + 1) * 512],
                    op=OP.add)
                bk.inc(inst, "sV", 1)
                EV[f"p6_ev_h{half}"] = bk.v("sV")
            VE(p6_ev)
        EV[f"p6_mm_{st}"] = bk.v("sT")
    EV["p6_done"] = bk.v("sV")

    if DBG:
        dump(dbg_x2, STRM[:, :])
        dump(dbg_xq, XQR[:, :])

        def _dln1(sy):
            bk.inc(sy.dma_start(out=dbg_ln1, in_=ln1sp.ap()), "d0", 16)
            bk.w(sy, "d0")
        SY(_dln1)

    for st in range(4):
        def ssrc2(c, st=st):
            return STRM[:, c * 2048 + st * 512: c * 2048 + (st + 1) * 512]
        ln_stats("ln2", st, ssrc2, ("sV", EV["p6_done"]))

        def nv2(v, st=st):
            bk.w(v, "sT", EV[f"ln2_rep_{st}"])
            for c in range(8):
                bk.w(v, "sV")
                bk.chain(v, v.tensor_tensor(out=SCR1[:, :], in0=ssrc2(c),
                                             in1=ps_sc0[:, 0:512], op=OP.mult), "sV")
                bk.inc(v.tensor_tensor(
                    out=LN2OWN[:, c * 2048 + st * 512: c * 2048 + (st + 1) * 512],
                    in0=SCR1[:, :], in1=ps_sc0[:, 512:1024], op=OP.subtract), "sV", 1)
            EV[f"ln2own_{st}"] = bk.v("sV")
        VE(nv2)

        def p7_send(g, st=st):
            bk.w(g, "sV", EV[f"ln2own_{st}"])
            inst = g.dma_start(
                out=fm3(agl_in.ap())[:, :, st * 512:(st + 1) * 512],
                in_=sb3(LN2OWN[:, :], 2048)[:, :, st * 512:(st + 1) * 512])
            bk.inc(inst, "d_am", 16)
        GE(p7_send)
    EV["ln2_done"] = bk.v("sV")

    def p7_exchange(g):
        bk.w(g, "d_am")
        inst = g.collective_compute("AllGather", OP.bypass, replica_groups=PAIRS,
                                    ins=[agl_in.ap().opt()], outs=[agl_out.ap().opt()])
        bk.inc(inst, "cc", 1)
        EV["p7_cc"] = bk.v("cc")
    GE(p7_exchange)

    def p7_spill(sy):
        bk.w(sy, "sV", EV["p6_done"])
        bk.inc(sy.dma_start(out=fm3(x2sp.ap()), in_=sb3(STRM[:, :], 2048)), "d_xsp", 16)
        EV["p7_sp"] = bk.v("d_xsp")
    SY(p7_spill)

    def p8_wload(g):
        bk.w(g, "sV", EV["p6_done"])
        bk.w(g, "d_xsp", EV["p7_sp"])
        bk.inc(g.dma_start(out=MBIG[:, :], in_=mbig), "d_mb2", 16)
        EV["mbig1"] = bk.v("d_mb2")
        for i, w_ in enumerate([wq1, wk1, wv1]):
            bk.inc(g.dma_start(out=sb3(WQKV1[:, i * 4096:(i + 1) * 4096], 512),
                               in_=fm3(w_)), "d_wk", 16)
        EV["p8_w"] = bk.v("d_wk")
    GE(p8_wload)

    def v_ones1(v):
        bk.w(v, "sT", bk.v("sT"))
        inst = v.memset(VTM[:, :].rearrange("p (c t) -> p c t", t=65)[:, :, 64:65], 1.0)
        bk.inc(inst, "sV", 1)
        EV["vones1"] = bk.v("sV")
    VE(v_ones1)

    for gqs in range(8):
        slot = gqs % 2

        def p8_ld(sy, gqs=gqs, slot=slot):
            bk.w(sy, "cc", EV["p7_cc"])
            bk.w(sy, "d_xsp", EV["p7_sp"])
            if f"p8_qkv_{gqs-2}" in EV:
                bk.w(sy, "sV", EV[f"p8_qkv_{gqs-2}"])
            blockr = gqs // 4
            coff = (gqs % 4) * 512
            srcv = fm3(agl_out.ap()[blockr * 1024:(blockr + 1) * 1024, coff:coff + 512])
            sem = f"dx{slot}"
            bk.inc(sy.dma_start(out=sb3(LN2FR[:, slot * 4096:(slot + 1) * 4096], 512),
                                in_=srcv), sem, 16)
            EV[f"p8_ld_{gqs}"] = (sem, bk.v(sem))
        SY(p8_ld)

        for hp in range(4):
            def p8_qk(t, gqs=gqs, hp=hp, slot=slot):
                bk.w(t, EV[f"p8_ld_{gqs}"][0], EV[f"p8_ld_{gqs}"][1])
                bk.w(t, "d_wk", EV["p8_w"])
                if "p8_qk_ev" in EV:
                    bk.w(t, "sV", EV["p8_qk_ev"])
                for fc in range(8):
                    t.matmul(ps_sc0[:, 0:512],
                             WQKV1[:, fc * 512 + hp * 128: fc * 512 + (hp + 1) * 128],
                             LN2FR[:, slot * 4096 + fc * 512: slot * 4096 + (fc + 1) * 512],
                             start=(fc == 0), stop=(fc == 7))
                    inst = t.matmul(ps_sc0[:, 512:1024],
                                    WQKV1[:, 4096 + fc * 512 + hp * 128: 4096 + fc * 512 + (hp + 1) * 128],
                                    LN2FR[:, slot * 4096 + fc * 512: slot * 4096 + (fc + 1) * 512],
                                    start=(fc == 0), stop=(fc == 7))
                bk.inc(inst, "sT", 1)
                EV["p8_qk_mm"] = bk.v("sT")
            TE(p8_qk)

            def p8_qk_ev(v, gqs=gqs, hp=hp):
                bk.w(v, "sT", EV["p8_qk_mm"])
                bk.inc(v.tensor_copy(out=QFM[:, hp * 4096 + gqs * 512: hp * 4096 + (gqs + 1) * 512],
                              in_=ps_sc0[:, 0:512]), "sV", 1)
                inst = v.tensor_copy(out=KFM[:, hp * 4096 + gqs * 512: hp * 4096 + (gqs + 1) * 512],
                                     in_=ps_sc0[:, 512:1024])
                bk.inc(inst, "sV", 1)
                EV["p8_qk_ev"] = bk.v("sV")
            VE(p8_qk_ev)

        for ts4 in range(4):
            def p8_v(t, gqs=gqs, ts4=ts4, slot=slot):
                bk.w(t, EV[f"p8_ld_{gqs}"][0], EV[f"p8_ld_{gqs}"][1])
                if "p8_v_ev" in EV:
                    bk.w(t, "sV", EV["p8_v_ev"])
                for fc in range(8):
                    inst = t.matmul(ps_sc1[:, 0:512],
                                    LN2FR[:, slot * 4096 + fc * 512 + ts4 * 128: slot * 4096 + fc * 512 + (ts4 + 1) * 128],
                                    WQKV1[:, 8192 + fc * 512: 8192 + (fc + 1) * 512],
                                    start=(fc == 0), stop=(fc == 7))
                bk.inc(inst, "sT", 1)
                EV["p8_v_mm"] = bk.v("sT")
            TE(p8_v)

            def p8_v_ev(v, gqs=gqs, ts4=ts4):
                bk.w(v, "sT", EV["p8_v_mm"])
                gts = gqs * 4 + ts4
                if gts < 31:
                    inst = v.tensor_copy(
                        out=vout(gts * 8, 8),
                        in_=ps_sc1[:, 0:512].rearrange("p (c t) -> p c t", t=64))
                    bk.inc(inst, "sV", 1)
                else:
                    bk.inc(v.tensor_copy(
                        out=vout(248, 4),
                        in_=ps_sc1[:, 0:256].rearrange("p (c t) -> p c t", t=64)), "sV", 1)
                    inst = v.tensor_copy(
                        out=vout(252, 4),
                        in_=ps_sc1[:, 256:512].rearrange("p (c t) -> p c t", t=64))
                    bk.inc(inst, "sV", 1)
                EV["p8_v_ev"] = bk.v("sV")
            VE(p8_v_ev)
        EV[f"p8_qkv_{gqs}"] = bk.v("sV")
    EV["p8_done"] = bk.v("sV")

    norms_flat("l1", QFM, KFM, ("sV", EV["p8_done"]))

    def conc_hook(hp, qs):
        if qs != 7:
            return

        def send(g, hp=hp):
            bk.w(g, "sV", EV["att_evac"])
            pid = nc.partition_id([mybir.EngineType.Pool])
            off = g.snap((1 - pid % 2) * 2048 + hp * 4096, min_val=hp * 4096,
                         max_val=hp * 4096 + 2048)
            bk.inc(g.dma_start(out=agc_in.ap()[hp * 128:(hp + 1) * 128, :],
                               in_=CONC[:, bass.ds(off, 2048)]), "d_p10", 16)
            ks = (0, 1) if hp == 2 else ((2, 3) if hp == 3 else ())
            for k in ks:
                ooff = g.snap((pid % 2) * 2048 + k * 4096, min_val=k * 4096,
                              max_val=k * 4096 + 2048)
                bk.inc(g.dma_start(out=CONCO[:, k * 2048:(k + 1) * 2048],
                                   in_=CONC[:, bass.ds(ooff, 2048)]), "d_p10", 16)
        GE(send)

    attention(1, CONC, rrep_mode=True, qs_end=conc_hook)
    if DBG:
        dump(dbg_conc, CONC[:, :])

    def p10_g(g):
        bk.w(g, "sV", EV["attn1_done"])
        bk.w(g, "d_p10")
        pid = nc.partition_id([mybir.EngineType.Pool])
        inst = g.collective_compute("AllGather", OP.bypass, replica_groups=PAIRS,
                                    ins=[agc_in.ap().opt()], outs=[agc_out.ap().opt()])
        bk.inc(inst, "cc", 1)
        bk.w(g, "cc")
        for k in range(4):
            roff = g.snap((1 - pid % 2) * 512 + k * 128, min_val=k * 128,
                          max_val=512 + k * 128)
            bk.inc(g.dma_start(out=CONCR[:, k * 2048:(k + 1) * 2048],
                               in_=agc_out.ap()[bass.ds(roff, 128), :]), "d_p10", 16)
        bk.w(g, "d_p10")
        bk.inc(g.dma_start(out=sb3(WOUT1[:, :], 1024), in_=fm3(wo1)), "d_p10", 16)
        bk.w(g, "d_p10")
        bk.inc(g.nop(), "sG", 1)
        EV["p10_x"] = bk.v("sG")
    GE(p10_g)

    def p10_reload(sy):
        bk.w(sy, "d_xsp", EV["p7_sp"])
        bk.w(sy, "sV", EV["attn1_done"])
        bk.inc(sy.dma_start(out=sb3(STRM[:, :], 2048), in_=fm3(x2sp.ap())), "d_rl", 16)
        EV["p10_rl"] = bk.v("d_rl")
    SY(p10_reload)

    for st in range(4):
        for oc in range(8):
            half = oc % 2
            pb = ps_sc0 if half == 0 else ps_sc1

            def p11_mm(t, st=st, oc=oc, pb=pb, half=half):
                if st == 0 and oc == 0:
                    bk.w(t, "sG", EV["p10_x"])
                if f"p11_ev_h{half}" in EV:
                    bk.w(t, "sV", EV[f"p11_ev_h{half}"])
                for fc in range(8):
                    src = CONCO if fc < 4 else CONCR
                    k = fc if fc < 4 else fc - 4
                    inst = t.matmul(pb[:, 0:512],
                                    WOUT1[:, fc * 1024 + oc * 128: fc * 1024 + (oc + 1) * 128],
                                    src[:, k * 2048 + st * 512: k * 2048 + (st + 1) * 512],
                                    start=(fc == 0), stop=(fc == 7))
                bk.inc(inst, "sT", 1)
                EV["p11_mm"] = bk.v("sT")
            TE(p11_mm)

            def p11_ev(v, st=st, oc=oc, pb=pb, half=half):
                bk.w(v, "sT", EV["p11_mm"])
                if st == 0 and oc == 0:
                    bk.w(v, "d_rl", EV["p10_rl"])
                inst = v.scalar_tensor_tensor(
                    out=STRM[:, oc * 2048 + st * 512: oc * 2048 + (st + 1) * 512],
                    in0=pb[:, 0:512], scalar=BO1[:, oc:oc + 1],
                    in1=STRM[:, oc * 2048 + st * 512: oc * 2048 + (st + 1) * 512],
                    op0=OP.add, op1=OP.add)
                bk.inc(inst, "sV", 1)
                EV[f"p11_ev_h{half}"] = bk.v("sV")
            VE(p11_ev)
    EV["p11_done"] = bk.v("sV")

    if DBG:
        dump(dbg_x3, STRM[:, :])

    def p12_r12(sy):
        bk.w(sy, "sV", EV["p11_done"])
        bk.w(sy, "d_rsp", EV["r12_sp"])
        r3 = r12sp.ap().rearrange("r c p t -> p r c t")
        r12v = R12[:, :].rearrange("p (r c t) -> p r c t", r=2, t=512)
        bk.inc(sy.dma_start(out=r12v, in_=r3), "d_r12", 16)
        EV["p12_r12"] = bk.v("d_r12")
    SY(p12_r12)

    def p12_bfc(g):
        bk.w(g, "sV", EV["p11_done"])
        bk.inc(g.dma_start(out=BFC[:, :], in_=bfc), "d_bfc", 16)
        EV["p12_bfc"] = bk.v("d_bfc")
    GE(p12_bfc)

    for st in range(4):
        def ssrc3(c, st=st):
            return STRM[:, c * 2048 + st * 512: c * 2048 + (st + 1) * 512]
        ln_stats("ln3", st, ssrc3, ("sV", EV["p11_done"]))

        def nv3(v, st=st):
            bk.w(v, "sT", EV[f"ln3_rep_{st}"])
            for c in range(8):
                bk.w(v, "sV")
                bk.chain(v, v.tensor_tensor(out=SCR1[:, :], in0=ssrc3(c),
                                             in1=ps_sc0[:, 0:512], op=OP.mult), "sV")
                bk.inc(v.tensor_tensor(
                    out=LN3[:, c * 2048 + st * 512: c * 2048 + (st + 1) * 512],
                    in0=SCR1[:, :], in1=ps_sc0[:, 512:1024], op=OP.subtract), "sV", 1)
        VE(nv3)
    EV["ln3_done"] = bk.v("sV")

    if DBG:
        dump(dbg_ln3, LN3[:, :])

        def _wr12(sy):
            bk.w(sy, "d_r12", EV["p12_r12"])
        SY(_wr12)
        dump(dbg_r12, R12[:, :])

    for tc in range(16):
        slot = tc % 2
        for grp in range(2):
            def p13_tr(t, tc=tc, grp=grp):
                if tc == 0 and grp == 0:
                    bk.w(t, "sV", EV["ln3_done"])
                if "p13_x3t" in EV:
                    bk.w(t, "sV", EV["p13_x3t"])
                for ci in range(4):
                    c = grp * 4 + ci
                    inst = t.transpose(ps_m1[:, ci * 128:(ci + 1) * 128],
                                       STRM[:, c * 2048 + tc * 128: c * 2048 + (tc + 1) * 128],
                                       IDF[:, :])
                bk.inc(inst, "sT", 1)
                EV["p13_tr"] = bk.v("sT")
            TE(p13_tr)

            def p13_trev(v, tc=tc, grp=grp, slot=slot):
                bk.w(v, "sT", EV["p13_tr"])
                if tc >= 2 and grp == 0:
                    bk.w(v, EV[f"p13_x4_{tc-2}"][0], EV[f"p13_x4_{tc-2}"][1])
                for ci in range(4):
                    c = grp * 4 + ci
                    bk.inc(v.tensor_copy(
                        out=X3TM[:, slot * 1024 + c * 128: slot * 1024 + (c + 1) * 128],
                        in_=ps_m1[:, ci * 128:(ci + 1) * 128]), "sV", 1)
                EV["p13_x3t"] = bk.v("sV")
            VE(p13_trev)
        EV[f"p13_x3t_done_{tc}"] = bk.v("sV")

        for oh in range(2):
            pb = ps_sc0 if oh == 0 else ps_sc1

            def p13_mm(t, tc=tc, oh=oh, pb=pb):
                if tc == 0 and oh == 0:
                    bk.w(t, "d_r12", EV["p12_r12"])
                if f"p13_ev_h{oh}" in EV:
                    bk.w(t, "sV", EV[f"p13_ev_h{oh}"])
                for fc in range(8):
                    inst = t.matmul(pb[:, 0:512],
                                    LN3[:, fc * 2048 + tc * 128: fc * 2048 + (tc + 1) * 128],
                                    R12[:, oh * 4096 + fc * 512: oh * 4096 + (fc + 1) * 512],
                                    start=(fc == 0), stop=(fc == 7))
                bk.inc(inst, "sT", 1)
                EV["p13_mm"] = bk.v("sT")
            TE(p13_mm)

            def p13_ev(v, tc=tc, oh=oh, pb=pb, slot=slot):
                bk.w(v, "sT", EV["p13_mm"])
                if tc == 0 and oh == 0:
                    bk.w(v, "d_bfc", EV["p12_bfc"])
                bk.w(v, "sV")
                bk.chain(v, v.tensor_tensor(out=SCR1[:, :], in0=pb[:, 0:512],
                                            in1=BFC[:, oh * 512:(oh + 1) * 512], op=OP.add), "sV")
                inst = v.tensor_tensor(
                    out=X4T[:, slot * 1024 + oh * 512: slot * 1024 + (oh + 1) * 512],
                    in0=SCR1[:, :],
                    in1=X3TM[:, slot * 1024 + oh * 512: slot * 1024 + (oh + 1) * 512],
                    op=OP.add)
                bk.inc(inst, "sV", 1)
                EV[f"p13_ev_h{oh}"] = bk.v("sV")
            VE(p13_ev)
        EV[f"p13_x4v_{tc}"] = bk.v("sV")

        def p13_out(sy, tc=tc, slot=slot):
            bk.w(sy, "sV", EV[f"p13_x4v_{tc}"])
            sem = "d_oa" if slot == 0 else "d_ob"
            bk.inc(sy.dma_start(out=out[tc * 128:(tc + 1) * 128, :],
                                in_=X4T[:, slot * 1024:(slot + 1) * 1024]), sem, 16)
            EV[f"p13_x4_{tc}"] = (sem, bk.v(sem))
        SY(p13_out)

    def fin(g):
        bk.w(g, "d_oa", bk.v("d_oa"))
        bk.w(g, "d_ob", bk.v("d_ob"))
        g.nop()
    GE(fin)

    stack.close()
    nc.compile()
    return nc


_NC_CACHE = {}


def _get_nc():
    if "nc" not in _NC_CACHE:
        _NC_CACHE["nc"] = build_nc()
    return _NC_CACHE["nc"]


def _host_inputs(c, a):
    b, p = c // 2, c % 2
    bf = ml_dtypes.bfloat16
    x = np.asarray(a["L0_attn_x"])
    xp = np.ascontiguousarray(x[b].T)
    perm = np.concatenate([np.arange(512 * p, 512 * p + 512),
                           np.arange(512 * (1 - p), 512 * (1 - p) + 512)])
    m = {}
    m["x0f"] = np.ascontiguousarray(xp[perm])
    m["x0o"] = np.ascontiguousarray(x[b, p * 2048:(p + 1) * 2048].T)
    m["wq0"] = np.ascontiguousarray(np.asarray(a["L0_attn_Wq"])[8 * p:8 * p + 8])
    m["wk0"] = np.ascontiguousarray(np.asarray(a["L0_attn_Wk"])[8 * p:8 * p + 8])
    m["wv0"] = np.ascontiguousarray(np.asarray(a["L0_attn_Wv"])[8 * p:8 * p + 8])
    m["wo0"] = np.ascontiguousarray(np.asarray(a["L0_attn_W_O"]))
    m["wf0"] = np.ascontiguousarray(np.asarray(a["L0_ffn_weight"]).T)
    m["bf0"] = np.ascontiguousarray(np.asarray(a["L0_ffn_bias"]).reshape(8, 128).T)
    w = np.asarray(a["L1_attn_w_qkv"])
    m["wq1"] = np.ascontiguousarray(w[:, 512 * p:512 * p + 512])
    m["wk1"] = np.ascontiguousarray(w[:, 1024 + 512 * p:1024 + 512 * p + 512])
    m["wv1"] = np.ascontiguousarray(w[:, 2048 + 512 * p:2048 + 512 * p + 512])
    m["wo1"] = np.ascontiguousarray(np.asarray(a["L1_attn_w_out"])[perm])
    m["bo1"] = np.ascontiguousarray(np.asarray(a["L1_attn_b_out"]).reshape(8, 128).T)
    g1r = np.asarray(a["L1_ffn_G1_R"]); g2r = np.asarray(a["L1_ffn_G2_R"])
    g1i = np.asarray(a["L1_ffn_G1_I"]); g2i = np.asarray(a["L1_ffn_G2_I"])
    m["uR"] = np.ascontiguousarray(np.stack(
        [np.repeat(np.repeat(g1r[0, :, ci, :].T, 16, 0), 16, 1) for ci in range(4)]),
        dtype=np.float32)
    m["vR"] = np.ascontiguousarray(np.stack(
        [np.tile(g2r[ci, :, 0, :].T, (32, 32)) for ci in range(4)]), dtype=np.float32)
    m["uI"] = np.ascontiguousarray(np.stack(
        [np.repeat(np.repeat(g1i[0, :, ci, :].T, 16, 0), 16, 1) for ci in range(4)]),
        dtype=np.float32)
    m["vI"] = np.ascontiguousarray(np.stack(
        [np.tile(g2i[ci, :, 0, :].T, (32, 32)) for ci in range(4)]), dtype=np.float32)
    m["bfc"] = np.ascontiguousarray(np.tile(np.concatenate(
        [np.asarray(a["L1_ffn_bias_real"]), np.asarray(a["L1_ffn_bias_imag"])])[None, :],
        (128, 1)).astype(np.float32))
    m["idb"] = np.eye(128, dtype=np.float32).astype(bf)
    m["idf"] = np.eye(128, dtype=np.float32)
    yy, xx = np.mgrid[0:128, 0:896]
    mb = ((xx - 384) >= yy).astype(np.float32).astype(bf)
    m["mbig"] = np.concatenate(
        [np.tile(mb[:, 384 - 128 * j:384 - 128 * j + 512], (1, 2)) for j in range(4)],
        axis=1)
    sh = np.zeros((128, 4), np.float32)
    sh[0:64, 0] = 1.0
    sh[64:128, 1] = 1.0
    m["selh"] = sh.astype(bf)
    cstv = np.zeros((128, 8), np.float32)
    cstv[:, 0] = 1e-12
    cstv[:, 1] = 16.0
    cstv[:, 4] = 1e-5
    m["cst"] = cstv
    return m


def kernel(**inputs):
    nc = _get_nc()
    in_maps = [_host_inputs(c, inputs) for c in range(NCORES)]
    res = run_bass_kernel_spmd(nc, in_maps, core_ids=list(range(NCORES)))
    outp = np.empty((B, S, E), dtype=np.float32)
    for c in range(NCORES):
        b, p = c // 2, c % 2
        outp[b, p * 2048:(p + 1) * 2048, :] = res.results[c]["out"]
    return outp


# version 19
# speedup vs baseline: 1.0169x; 1.0169x over previous
import contextlib

import numpy as np
import ml_dtypes

import concourse.bacc as bacc
import concourse.bass as bass
import concourse.mybir as mybir
from concourse.bass_utils import run_bass_kernel_spmd
from concourse.hw_specs import get_activation_tables

F32 = mybir.dt.float32
BF16 = mybir.dt.bfloat16
I32 = mybir.dt.int32
AF = mybir.ActivationFunctionType
OP = mybir.AluOpType
AX = mybir.AxisListType

B, S, E = 4, 4096, 1024
TOWN = 2048
NCORES = 8
LN2C = float(np.log(2.0))
INVLN2 = float(1.0 / np.log(2.0))

DBG = False

SB_BASE = 16512
SMALL_SZ = 16192
BIG1 = SB_BASE + SMALL_SZ
BIG2 = BIG1 + 32768
BIG3 = BIG2 + 32768
BIG4 = BIG3 + 32768
BIG5 = BIG4 + 32768
assert BIG5 + 65536 <= 229344


class BK:
    def __init__(self):
        self.c = {}
        self.h = {}

    def add(self, name, handle):
        self.h[name] = handle
        self.c[name] = 0

    def inc(self, inst, name, n=1):
        inst.then_inc(self.h[name], n)
        self.c[name] += n
        return self.c[name]

    def w(self, eng, name, val=None):
        eng.wait_ge(self.h[name], self.c[name] if val is None else val)

    def v(self, name):
        return self.c[name]

    def chain(self, eng, inst, name):
        self.inc(inst, name, 1)
        eng.wait_ge(self.h[name], self.c[name])


def build_nc():
    nc = bacc.Bacc("TRN2", target_bir_lowering=False)
    ACT_SET_LNEXP = next(
        i for i, fs in enumerate(get_activation_tables(nc.m.arch).values())
        if AF.Ln in fs and AF.Exp in fs)

    def din(name, shape, dtype):
        return nc.dram_tensor(name, shape, dtype, kind="ExternalInput").ap()

    x0f = din("x0f", [E, S], F32)
    x0o = din("x0o", [E, TOWN], F32)
    wq0 = din("wq0", [8, 64, 64], F32)
    wk0 = din("wk0", [8, 64, 64], F32)
    wv0 = din("wv0", [8, 64, 64], F32)
    wo0 = din("wo0", [E, E], F32)
    wf0 = din("wf0", [E, E], F32)
    bf0 = din("bf0", [128, 8], F32)
    wq1 = din("wq1", [E, 512], F32)
    wk1 = din("wk1", [E, 512], F32)
    wv1 = din("wv1", [E, 512], F32)
    wo1 = din("wo1", [E, E], F32)
    bo1 = din("bo1", [128, 8], F32)
    uRd = din("uR", [4, 512, 512], F32)
    vRd = din("vR", [4, 512, 512], F32)
    uId = din("uI", [4, 512, 512], F32)
    vId = din("vI", [4, 512, 512], F32)
    bfc = din("bfc", [128, 1024], F32)
    idb = din("idb", [128, 128], BF16)
    idf = din("idf", [128, 128], F32)
    mbig = din("mbig", [128, 896], BF16)
    selh = din("selh", [128, 4], BF16)
    cst = din("cst", [128, 8], F32)
    out = nc.dram_tensor("out", [TOWN, E], F32, kind="ExternalOutput").ap()
    if DBG:
        dbg_ln0 = nc.dram_tensor("dbg_ln0", [128, 16384], BF16, kind="ExternalOutput").ap()
        dbg_cat = nc.dram_tensor("dbg_cat", [128, 16384], BF16, kind="ExternalOutput").ap()
        dbg_scrm = nc.dram_tensor("dbg_scrm", [128, 16384], BF16, kind="ExternalOutput").ap()
        dbg_x1 = nc.dram_tensor("dbg_x1", [128, 16384], F32, kind="ExternalOutput").ap()
        dbg_x2 = nc.dram_tensor("dbg_x2", [128, 16384], F32, kind="ExternalOutput").ap()
        dbg_x3 = nc.dram_tensor("dbg_x3", [128, 16384], F32, kind="ExternalOutput").ap()
        dbg_conc = nc.dram_tensor("dbg_conc", [128, 16384], BF16, kind="ExternalOutput").ap()
        dbg_qfm = nc.dram_tensor("dbg_qfm", [128, 16384], BF16, kind="ExternalOutput").ap()
        dbg_kfm = nc.dram_tensor("dbg_kfm", [128, 16384], BF16, kind="ExternalOutput").ap()
        dbg_vtm = nc.dram_tensor("dbg_vtm", [128, 16380], BF16, kind="ExternalOutput").ap()
        dbg_xq = nc.dram_tensor("dbg_xq", [128, 8192], BF16, kind="ExternalOutput").ap()
        dbg_rdt = nc.dram_tensor("dbg_rdt", [128, 256], F32, kind="ExternalOutput").ap()
        dbg_wqt = nc.dram_tensor("dbg_wqt", [128, 8192], BF16, kind="ExternalOutput").ap()
        dbg_sc = nc.dram_tensor("dbg_sc", [128, 4], F32, kind="ExternalOutput").ap()
        dbg_ln1 = nc.dram_tensor("dbg_ln1", [8, 128, 2048], F32, kind="ExternalOutput").ap()
        dbg_ln3 = nc.dram_tensor("dbg_ln3", [128, 16384], BF16, kind="ExternalOutput").ap()
        dbg_r12 = nc.dram_tensor("dbg_r12", [128, 8192], BF16, kind="ExternalOutput").ap()
        dbg_x3t = nc.dram_tensor("dbg_x3t", [128, 2048], F32, kind="ExternalOutput").ap()

    x2sp = nc.dram_tensor("x2sp", [E, TOWN], F32)
    ln1sp = nc.dram_tensor("ln1sp", [8, 128, TOWN], F32)
    r12sp = nc.dram_tensor("r12sp", [2, 8, 128, 512], BF16)
    am_in = nc.dram_tensor("am_in", [1, 8], F32)
    am_out = nc.dram_tensor("am_out", [1, 8], F32)
    agl_in = nc.dram_tensor("agl_in", [E, TOWN], BF16)
    agl_out = nc.dram_tensor("agl_out", [2 * E, TOWN], BF16)
    agc_in = nc.dram_tensor("agc_in", [512, TOWN], BF16)
    agc_out = nc.dram_tensor("agc_out", [1024, TOWN], BF16)

    PAIRS = [[0, 1], [2, 3], [4, 5], [6, 7]]
    bk = BK()
    EV = {}

    def fm3(dram2d):
        return dram2d.rearrange("(c p) t -> p c t", p=128)

    def sb3(sbslice, t):
        return sbslice.rearrange("p (c t) -> p c t", t=t)

    def sb(name, shape, dtype, off):
        return nc.alloc_sbuf_tensor_at(name, shape, dtype, offset=off)

    o = SB_BASE

    def nxt(n):
        nonlocal o
        r = o
        o += (n + 31) // 32 * 32
        return r

    IDB = sb("IDB", [128, 128], BF16, nxt(256))
    IDF = sb("IDF", [128, 128], F32, nxt(512))
    SELH = sb("SELH", [128, 4], BF16, nxt(32))
    VTX = sb("VTX", [128, 260], BF16, nxt(520))
    ONESR = sb("ONESR", [65, 64], F32, nxt(256))
    CST = sb("CST", [128, 8], F32, nxt(32))
    BF0 = sb("BF0", [128, 8], F32, nxt(32))
    BO1 = sb("BO1", [128, 8], F32, nxt(32))
    ONESF = sb("ONESF", [128, 1], F32, nxt(32))
    ONESB = sb("ONESB", [128, 1], BF16, nxt(32))
    ONES1R = sb("ONES1R", [1, 128], F32, nxt(512))
    WQ0 = sb("WQ0", [128, 256], BF16, nxt(512))
    WK0 = sb("WK0", [128, 256], BF16, nxt(512))
    WVD0 = sb("WVD0", [128, 512], BF16, nxt(1024))
    BIASM = sb("BIASM", [128, 4], F32, nxt(32))
    XBF = sb("XBF", [128, 1024], BF16, nxt(2048))
    XSQ = sb("XSQ", [128, 1024], BF16, nxt(2048))
    STATA = sb("STATA", [33, 512], F32, nxt(2048))
    STATB = sb("STATB", [33, 512], F32, nxt(2048))
    NRMC = sb("NRMC", [34, 16], F32, nxt(64))
    NRT = sb("NRT", [1, 64], F32, nxt(256))
    AMXC = sb("AMXC", [128, 32], F32, nxt(128))
    SCR1 = sb("SCR1", [128, 512], F32, nxt(2048))
    SCAL = sb("SCAL", [1, 8], F32, nxt(32))
    SCXV = sb("SCXV", [128, 1], F32, nxt(32))
    SCWV = sb("SCWV", [128, 1], F32, nxt(32))
    DSCV = sb("DSCV", [128, 1], F32, nxt(32))
    SC1 = sb("SC1", [1, 16], F32, nxt(64))
    assert o <= BIG1, o

    QSQ = sb("QSQ", [128, 512], BF16, BIG1)
    BFC = sb("BFC", [128, 1024], BF16, BIG2)
    PT = sb("PT", [128, 8192], BF16, BIG5 + 32768)
    RDT = sb("RDT", [128, 256], F32, BIG5 + 49152)
    RDTL = sb("RDTL", [128, 256], F32, BIG5 + 50176)
    S1D = sb("S1D", [65, 1024], F32, BIG5 + 49152)
    RDD = sb("RDD", [65, 1024], F32, BIG5 + 53248)
    RREPS = sb("RREPS", [128, 512], F32, BIG5 + 14336)
    MBIG = sb("MBIG", [128, 896], BF16, BIG5 + 57344)
    LN0 = sb("LN0", [128, 16384], BF16, BIG1)
    CAT = sb("CAT", [128, 16384], BF16, BIG1)
    X0OR = sb("X0OR", [128, 8192], F32, BIG1)
    CONC = sb("CONC", [128, 16384], BF16, BIG1)
    WOUT1 = sb("WOUT1", [128, 8192], BF16, BIG1)

    QFM = sb("QFM", [128, 16384], BF16, BIG2)
    SCRM = sb("SCRM", [128, 16384], BF16, BIG2)

    KFM = sb("KFM", [128, 16384], BF16, BIG3)
    WF32 = sb("WF32", [128, 8192], F32, BIG3)
    WQT = sb("WQT", [128, 8192], BF16, BIG3)
    LN1R = sb("LN1R", [128, 1024], F32, BIG3 + 16384)
    LN1T = sb("LN1T", [128, 1024], F32, BIG3 + 16384 + 4096)
    LN2OWN = sb("LN2OWN", [128, 16384], BF16, BIG3)
    CONCR = sb("CONCR", [128, 8192], BF16, BIG3)
    CONCO = sb("CONCO", [128, 8192], BF16, BIG3 + 16384)
    LN3 = sb("LN3", [128, 16384], BF16, BIG3)

    VTM = sb("VTM", [128, 16380], BF16, BIG4)

    def vsl(kbd, hp, h):
        gi = kbd * 8 + hp * 2 + h
        if gi < 252:
            return VTM[:, gi * 65:(gi + 1) * 65]
        return VTX[:, (gi - 252) * 65:(gi - 251) * 65]

    def vout(gi0, n):
        if gi0 < 252:
            base = VTM[:, gi0 * 65:(gi0 + n) * 65]
        else:
            base = VTX[:, (gi0 - 252) * 65:(gi0 - 252 + n) * 65]
        return base.rearrange("p (c t) -> p c t", t=65)[:, :, 0:64]
    WOB = sb("WOB", [128, 8192], BF16, BIG4)
    QA = sb("QA", [128, 512], F32, BIG4 + 16384)
    QZ = sb("QZ", [128, 512], F32, BIG4 + 16384 + 2048)
    QE = sb("QE", [128, 512], I32, BIG4 + 16384 + 4096)
    QT = sb("QT", [128, 512], I32, BIG4 + 16384 + 6144)
    QM = sb("QM", [128, 512], I32, BIG4 + 16384 + 8192)
    QTC = sb("QTC", [128, 512], F32, BIG4 + 16384 + 10240)
    QSG = sb("QSG", [128, 512], F32, BIG4 + 16384 + 12288)
    ONESI = sb("ONESI", [128, 512], I32, BIG4 + 16384 + 14336)
    QA1 = sb("QA1", [128, 512], F32, BIG1)
    QZ1 = sb("QZ1", [128, 512], F32, BIG1 + 2048)
    QE1 = sb("QE1", [128, 512], I32, BIG1 + 4096)
    QSG1 = sb("QSG1", [128, 512], F32, BIG1 + 6144)
    XQR = sb("XQR", [128, 8192], BF16, BIG4)
    R12 = sb("R12", [128, 8192], BF16, BIG4)
    X3TM = sb("X3TM", [128, 2048], F32, BIG4 + 16384)
    X4T = sb("X4T", [128, 2048], F32, BIG4 + 16384 + 8192)

    X0FR = sb("X0FR", [128, 8192], F32, BIG5)
    LN2FR = sb("LN2FR", [128, 8192], BF16, BIG5)
    WQKV1 = sb("WQKV1", [128, 12288], BF16, BIG5 + 16384)
    STRM = sb("STRM", [128, 16384], F32, BIG5)
    UVU = sb("UVU", [128, 1024], F32, BIG5 + 32768)
    UVV = sb("UVV", [128, 1024], F32, BIG5 + 32768 + 4096)
    WTACC = sb("WTACC", [128, 2048], F32, BIG5 + 32768 + 8192)
    R12B = sb("R12B", [128, 8192], BF16, BIG5 + 49152)

    ps_sc0 = nc.alloc_psum_tensor("ps_sc0", [128, 1024], F32)
    ps_sc1 = nc.alloc_psum_tensor("ps_sc1", [128, 1024], F32)
    ps_avd = nc.alloc_psum_tensor("ps_avd", [128, 1024], F32)
    ps_av = ps_avd[:, 0:512]
    ps_den = ps_avd[:, 512:1024]
    ps_m1 = nc.alloc_psum_tensor("ps_m1", [128, 512], F32)
    ps_tb = nc.alloc_psum_tensor("ps_tb", [128, 1024], BF16)

    stack = contextlib.ExitStack()
    for name in ["d0", "dx0", "dx1", "d_rl", "d_r12", "d_oa", "d_ob", "d_mb",
                 "d_mb2", "d_wk", "d_uv0", "d_uv1", "d_rsp", "d_wf", "d_sp0",
                 "d_sp1", "d_xsp", "d_wo", "d_gw", "d_am", "d_p10", "d_bfc",
                 "cc", "sT", "sS", "sV", "sG"]:
        bk.add(name, stack.enter_context(nc.semaphore(name)))
    block = stack.enter_context(nc.Block())

    TE = block.tensor

    def SE(fn):
        def _wrap(s):
            bk.w(s, "sS")
            fn(s)
        block.scalar(_wrap)

    def VE(fn):
        def _wrap(v):
            bk.w(v, "sV")
            fn(v)
        block.vector(_wrap)

    GE = block.gpsimd
    SY = block.sync

    def dump(name_ap, src_sb):
        if not DBG:
            return
        def _d(sy, name_ap=name_ap, src_sb=src_sb):
            bk.w(sy, "sV")
            bk.w(sy, "sS")
            bk.w(sy, "sT")
            bk.inc(sy.dma_start(out=name_ap, in_=src_sb), "d0", 16)
            bk.w(sy, "d0")
        SY(_d)
        gate = bk.v("d0")
        VE(lambda v: bk.w(v, "d0", gate))
        SE(lambda s: bk.w(s, "d0", gate))
        TE(lambda t: bk.w(t, "d0", gate))

        def _g(g, gate=gate):
            g.wait_ge(bk.h["d0"], gate)
        GE(_g)

    if DBG:
        EVD = {}

    def p0(sy):
        for dst, src in [
            (IDB[:, :], idb), (IDF[:, :], idf),
            (SELH[:, :], selh), (CST[:, :], cst),
            (BF0[:, :], bf0), (BO1[:, :], bo1),
        ]:
            bk.inc(sy.dma_start(out=dst, in_=src), "d0", 16)
        EV["p0_sy"] = bk.v("d0")
    SY(p0)

    def s_pretbl(s):
        inst = s.add_instruction(mybir.InstLoadActFuncSet(
            name=nc.get_next_instruction_name(),
            act_func_set_id=ACT_SET_LNEXP, ins=[], outs=[]))
        bk.inc(inst, "sS", 1)
    SE(s_pretbl)

    def v_ones(v):
        v.memset(ONESR[:, :], 1.0)
        v.memset(VTM[:, :].rearrange("p (c t) -> p c t", t=65)[:, :, 64:65], 1.0)
        inst = v.memset(VTX[:, :].rearrange("p (c t) -> p c t", t=65)[:, :, 64:65], 1.0)
        bk.inc(inst, "sV", 1)
        EV["vones0"] = bk.v("sV")
    VE(v_ones)

    def p0g(g):
        g.memset(ONESF[:, :], 1.0)
        g.memset(ONESB[:, :], 1.0)
        g.memset(ONES1R[:, :], 1.0)
        inst = g.memset(WVD0[:, :], 0.0)
        bk.inc(inst, "sG", 1)
        g.wait_ge(bk.h["sG"], bk.v("sG"))
        for hp in range(4):
            bk.inc(g.dma_start(out=WQ0[0:64, hp * 64:hp * 64 + 64], in_=wq0[2 * hp]), "d_gw", 16)
            bk.inc(g.dma_start(out=WQ0[64:128, hp * 64:hp * 64 + 64], in_=wq0[2 * hp + 1]), "d_gw", 16)
            bk.inc(g.dma_start(out=WK0[0:64, hp * 64:hp * 64 + 64], in_=wk0[2 * hp]), "d_gw", 16)
            bk.inc(g.dma_start(out=WK0[64:128, hp * 64:hp * 64 + 64], in_=wk0[2 * hp + 1]), "d_gw", 16)
            bk.inc(g.dma_start(out=WVD0[0:64, hp * 128:hp * 128 + 64], in_=wv0[2 * hp]), "d_gw", 16)
            bk.inc(g.dma_start(out=WVD0[64:128, hp * 128 + 64:hp * 128 + 128], in_=wv0[2 * hp + 1]), "d_gw", 16)
        inst = g.nop()
        bk.inc(inst, "sG", 1)
        EV["p0_g"] = bk.v("sG")
        EV["p0_gd"] = bk.v("d_gw")
    GE(p0g)

    def ln_stats(tag, st, src_of, ev_pre):
        for c in range(8):
            cs = c % 2

            def sc_(s, c=c, cs=cs):
                if c == 0 and ev_pre is not None:
                    bk.w(s, ev_pre[0], ev_pre[1])
                if f"{tag}_smm_{st}_{c-2}" in EV:
                    bk.w(s, "sT", EV[f"{tag}_smm_{st}_{c-2}"])
                bk.chain(s, s.activation(out=XBF[:, cs * 512:(cs + 1) * 512],
                                         in_=src_of(c), func=AF.Identity), "sS")
                inst = s.activation(out=XSQ[:, cs * 512:(cs + 1) * 512],
                                    in_=src_of(c), func=AF.Square)
                bk.inc(inst, "sS", 1)
                EV[f"{tag}_cast_{st}_{c}"] = bk.v("sS")
            SE(sc_)

            def tc_(t, c=c, cs=cs):
                bk.w(t, "sS", EV[f"{tag}_cast_{st}_{c}"])
                if c == 0:
                    bk.w(t, "sG", EV["p0_g"])
                    bk.w(t, "d_gw", EV["p0_gd"])
                    bk.w(t, "d0", EV["p0_sy"])
                t.matmul(ps_den[0:1, :], ONESB[:, 0:1], XBF[:, cs * 512:(cs + 1) * 512],
                         start=(c == 0), stop=(c == 7))
                inst = t.matmul(ps_m1[0:1, :], ONESB[:, 0:1], XSQ[:, cs * 512:(cs + 1) * 512],
                                start=(c == 0), stop=(c == 7))
                bk.inc(inst, "sT", 1)
                EV[f"{tag}_smm_{st}_{c}"] = bk.v("sT")
            TE(tc_)

        def post_v(v):
            bk.w(v, "sT", EV[f"{tag}_smm_{st}_7"])
            bk.inc(v.tensor_scalar(out=STATA[0:1, :], in0=ps_den[0:1, :], scalar1=1.0 / 1024.0,
                            scalar2=None, op0=OP.mult), "sV", 1)
            bk.chain(v, v.tensor_scalar(out=STATB[0:1, :], in0=ps_m1[0:1, :],
                                        scalar1=1.0 / 1024.0, scalar2=None, op0=OP.mult), "sV")
            bk.chain(v, v.tensor_tensor(out=SCR1[0:1, :], in0=STATA[0:1, :],
                                        in1=STATA[0:1, :], op=OP.mult), "sV")
            inst = v.tensor_tensor(out=STATB[0:1, :], in0=STATB[0:1, :],
                                   in1=SCR1[0:1, :], op=OP.subtract)
            bk.inc(inst, "sV", 1)
            EV[f"{tag}_var_{st}"] = bk.v("sV")
        VE(post_v)

        def post_s(s):
            bk.w(s, "sV", EV[f"{tag}_var_{st}"])
            bk.chain(s, s.activation(out=SCR1[0:1, :], in_=STATB[0:1, :], func=AF.Ln,
                                      bias=CST[0:1, 4:5]), "sS")
            inst = s.activation(out=STATB[0:1, :], in_=SCR1[0:1, :], func=AF.Exp, scale=-0.5)
            bk.inc(inst, "sS", 1)
            EV[f"{tag}_rstd_{st}"] = bk.v("sS")
        SE(post_s)

        def post_v3(v):
            bk.w(v, "sS", EV[f"{tag}_rstd_{st}"])
            bk.w(v, "sV")
            inst = v.tensor_tensor(out=STATA[0:1, :], in0=STATA[0:1, :],
                                   in1=STATB[0:1, :], op=OP.mult)
            bk.inc(inst, "sV", 1)
            EV[f"{tag}_brow_{st}"] = bk.v("sV")
        VE(post_v3)

        def rep_t(t):
            bk.w(t, "sV", EV[f"{tag}_brow_{st}"])
            t.matmul(ps_sc0[:, 0:512], ONES1R[:, :], STATB[0:1, :], start=True, stop=True)
            inst = t.matmul(ps_sc0[:, 512:1024], ONES1R[:, :], STATA[0:1, :],
                            start=True, stop=True)
            bk.inc(inst, "sT", 1)
            EV[f"{tag}_rep_{st}"] = bk.v("sT")
        TE(rep_t)


    for st in range(8):
        slot = st % 2

        def p1_load(sy, st=st, slot=slot):
            if st >= 2:
                bk.w(sy, "sV", EV[f"p1_qkv_{st-2}"])
            sem = f"dx{slot}"
            bk.inc(sy.dma_start(out=sb3(X0FR[:, slot * 4096:(slot + 1) * 4096], 512),
                                in_=fm3(x0f[:, st * 512:(st + 1) * 512])), sem, 16)
            EV[f"p1_x0_{st}"] = (sem, bk.v(sem))
        SY(p1_load)

        def xsl(c, slot=slot):
            return X0FR[:, slot * 4096 + c * 512: slot * 4096 + (c + 1) * 512]

        ln_stats("p1", st, xsl, EV[f"p1_x0_{st}"])

        def p1_norm(v, st=st, slot=slot):
            bk.w(v, "sT", EV[f"p1_rep_{st}"])
            for c in range(4):
                bk.w(v, "sV")
                bk.chain(v, v.tensor_tensor(out=SCR1[:, :], in0=xsl(c),
                                            in1=ps_sc0[:, 0:512], op=OP.mult), "sV")
                bk.inc(v.tensor_tensor(out=LN0[:, c * 4096 + st * 512: c * 4096 + (st + 1) * 512],
                                        in0=SCR1[:, :], in1=ps_sc0[:, 512:1024],
                                        op=OP.subtract), "sV", 1)
            EV[f"p1_ln0_{st}"] = bk.v("sV")
        VE(p1_norm)

        for hp in range(4):
            def tq(t, st=st, hp=hp):
                bk.w(t, "sV", EV[f"p1_ln0_{st}"])
                if st == 0 and hp == 0:
                    bk.w(t, "d_gw", EV["p0_gd"])
                if "p1_qk_ev" in EV:
                    bk.w(t, "sV", EV["p1_qk_ev"])
                t.matmul(ps_sc1[0:64, 0:512], WQ0[0:64, hp * 64:hp * 64 + 64],
                         LN0[0:64, hp * 4096 + st * 512: hp * 4096 + (st + 1) * 512],
                         start=True, stop=True, tile_position=(0, 0), skip_group_check=True)
                t.matmul(ps_sc1[64:128, 0:512], WQ0[64:128, hp * 64:hp * 64 + 64],
                         LN0[64:128, hp * 4096 + st * 512: hp * 4096 + (st + 1) * 512],
                         start=True, stop=True, tile_position=(64, 64), skip_group_check=True)
                t.matmul(ps_sc1[0:64, 512:1024], WK0[0:64, hp * 64:hp * 64 + 64],
                         LN0[0:64, hp * 4096 + st * 512: hp * 4096 + (st + 1) * 512],
                         start=True, stop=True, tile_position=(0, 0), skip_group_check=True)
                inst = t.matmul(ps_sc1[64:128, 512:1024], WK0[64:128, hp * 64:hp * 64 + 64],
                                LN0[64:128, hp * 4096 + st * 512: hp * 4096 + (st + 1) * 512],
                                start=True, stop=True, tile_position=(64, 64),
                                skip_group_check=True)
                bk.inc(inst, "sT", 1)
                EV["p1_qk_mm"] = bk.v("sT")
            TE(tq)

            def vq(v, st=st, hp=hp):
                bk.w(v, "sT", EV["p1_qk_mm"])
                bk.inc(v.tensor_copy(out=QFM[:, hp * 4096 + st * 512: hp * 4096 + (st + 1) * 512],
                              in_=ps_sc1[:, 0:512]), "sV", 1)
                inst = v.tensor_copy(out=KFM[:, hp * 4096 + st * 512: hp * 4096 + (st + 1) * 512],
                                     in_=ps_sc1[:, 512:1024])
                bk.inc(inst, "sV", 1)
                EV["p1_qk_ev"] = bk.v("sV")
            VE(vq)

            def tv(t, st=st, hp=hp):
                if "p1_v_ev" in EV:
                    bk.w(t, "sV", EV["p1_v_ev"])
                bk.w(t, "sV", EV[f"p1_ln0_{st}"])
                for ts4 in range(4):
                    gts = st * 4 + ts4
                    inst = t.matmul(ps_av[:, 0:512][:, ts4 * 128:(ts4 + 1) * 128],
                                    LN0[:, hp * 4096 + gts * 128: hp * 4096 + (gts + 1) * 128],
                                    WVD0[:, hp * 128:(hp + 1) * 128],
                                    start=True, stop=True)
                bk.inc(inst, "sT", 1)
                EV["p1_v_mm"] = bk.v("sT")
            TE(tv)

            def vv(v, st=st, hp=hp):
                bk.w(v, "sT", EV["p1_v_mm"])
                for ts4 in range(4):
                    gts = st * 4 + ts4
                    bk.inc(v.tensor_copy(
                        out=vout(gts * 8 + hp * 2, 2),
                        in_=ps_av[:, ts4 * 128:(ts4 + 1) * 128].rearrange(
                            "p (c t) -> p c t", t=64)), "sV", 1)
                EV["p1_v_ev"] = bk.v("sV")
            VE(vv)
        EV[f"p1_qkv_{st}"] = bk.v("sV")
    EV["p1_done"] = bk.v("sV")

    def norms_flat(tag, qbuf, kbuf, ev_ready):
        for hp in range(4):
            for st in range(8):
                def vn(v, hp=hp, st=st):
                    bk.w(v, ev_ready[0], ev_ready[1])
                    if "nrm_mmk" in EV:
                        bk.w(v, "sT", EV["nrm_mmk"])
                    if hp == 0 and st == 0:
                        bk.inc(v.memset(ps_m1[0:34, :], 0.0), "sV", 1)
                    inst = v.tensor_tensor(
                        out=QSQ[:, :],
                        in0=qbuf[:, hp * 4096 + st * 512: hp * 4096 + (st + 1) * 512],
                        in1=qbuf[:, hp * 4096 + st * 512: hp * 4096 + (st + 1) * 512],
                        op=OP.mult)
                    bk.inc(inst, "sV", 1)
                    EV["nrm_sq"] = bk.v("sV")
                VE(vn)

                def tn(t, hp=hp, st=st):
                    bk.w(t, "sV", EV["nrm_sq"])
                    inst = t.matmul(ps_m1[0:2, :], SELH[:, 0:2], QSQ[:, :],
                                    start=True, stop=True, tile_position=(0, 0),
                                    skip_group_check=True)
                    bk.inc(inst, "sT", 1)
                    EV["nrm_mmq"] = bk.v("sT")
                TE(tn)

                def vn2(v, hp=hp, st=st):
                    bk.w(v, "sT", EV["nrm_mmq"])
                    inst = v.tensor_tensor(
                        out=QSQ[:, :],
                        in0=kbuf[:, hp * 4096 + st * 512: hp * 4096 + (st + 1) * 512],
                        in1=kbuf[:, hp * 4096 + st * 512: hp * 4096 + (st + 1) * 512],
                        op=OP.mult)
                    bk.inc(inst, "sV", 1)
                    EV["nrm_sqk"] = bk.v("sV")
                VE(vn2)

                def tn2(t, hp=hp, st=st):
                    bk.w(t, "sV", EV["nrm_sqk"])
                    inst = t.matmul(ps_m1[32:34, :], SELH[:, 0:2], QSQ[:, :],
                                    start=True, stop=True, tile_position=(0, 32),
                                    skip_group_check=True)
                    bk.inc(inst, "sT", 1)
                    EV["nrm_mmk"] = bk.v("sT")
                TE(tn2)

                def vn3(v, hp=hp, st=st):
                    bk.w(v, "sT", EV["nrm_mmk"])
                    inst = v.reduce_max(out=NRMC[0:34, st:st + 1], in_=ps_m1[0:34, :], axis=AX.X)
                    bk.inc(inst, "sV", 1)
                    EV["nrm_red"] = bk.v("sV")
                VE(vn3)

            def vfin(v, hp=hp):
                bk.w(v, "sV")
                inst = v.reduce_max(out=NRMC[0:34, 8:9], in_=NRMC[0:34, 0:8], axis=AX.X)
                bk.inc(inst, "sV", 1)
                EV["nrm_fin"] = bk.v("sV")
            VE(vfin)

            def tfin(t, hp=hp):
                bk.w(t, "sV", EV["nrm_fin"])
                inst = t.transpose(ps_m1[0:1, 64:98], NRMC[0:34, 8:9], IDF[0:34, 0:34])
                bk.inc(inst, "sT", 1)
                EV["nrm_tr"] = bk.v("sT")
            TE(tfin)

            def vfin2(v, hp=hp, tag=tag):
                bk.w(v, "sT", EV["nrm_tr"])
                if hp > 0:
                    bk.w(v, "sG", EV[f"biasm_{tag}_{hp-1}"])
                bk.chain(v, v.tensor_copy(out=NRT[0:1, 0:34], in_=ps_m1[0:1, 64:98]), "sV")
                bk.chain(v, v.tensor_tensor(out=NRT[0:1, 40:42], in0=NRT[0:1, 0:2],
                                            in1=NRT[0:1, 32:34], op=OP.add), "sV")
                bk.chain(v, v.reduce_max(out=NRT[0:1, 48:49], in_=NRT[0:1, 40:42], axis=AX.X,
                                         negate=True), "sV")
                inst = v.tensor_scalar(out=NRT[0:1, 49:50], in0=NRT[0:1, 48:49],
                                       scalar1=1.0 / 16.0, scalar2=None, op0=OP.mult)
                bk.inc(inst, "sV", 1)
                EV["nrm_m"] = bk.v("sV")
            VE(vfin2)

            def gfin(g, hp=hp, tag=tag):
                bk.w(g, "sV", EV["nrm_m"])
                inst = g.partition_broadcast(BIASM[:, hp:hp + 1], NRT[0:1, 49:50])
                bk.inc(inst, "sG", 1)
                EV[f"biasm_{tag}_{hp}"] = bk.v("sG")
            GE(gfin)

    if DBG:
        dump(dbg_ln0, LN0[:, :])
        dump(dbg_qfm, QFM[:, :])
        dump(dbg_kfm, KFM[:, :])
        dump(dbg_vtm, VTM[:, :])
    norms_flat("l0", QFM, KFM, ("sV", EV["p1_done"]))

    for which in range(2):
        usrc = uRd if which == 0 else uId
        vsrc = vRd if which == 0 else vId
        for c in range(4):
            for ch in range(4):
                cs = ch % 2

                def ld(sy, c=c, ch=ch, cs=cs, usrc=usrc, vsrc=vsrc):
                    if f"r12_mul_m2" in EV:
                        bk.w(sy, "sV", EV["r12_mul_m2"])
                    sem = f"d_uv{cs}"
                    bk.inc(sy.dma_start(out=UVU[:, cs * 512:(cs + 1) * 512],
                                        in_=usrc[c][ch * 128:(ch + 1) * 128, :]), sem, 16)
                    bk.inc(sy.dma_start(out=UVV[:, cs * 512:(cs + 1) * 512],
                                        in_=vsrc[c][ch * 128:(ch + 1) * 128, :]), sem, 16)
                    EV["r12_ld"] = (sem, bk.v(sem))
                SY(ld)

                def mul(v, c=c, ch=ch, cs=cs):
                    bk.w(v, EV["r12_ld"][0], EV["r12_ld"][1])
                    if c == 0:
                        inst = v.tensor_tensor(out=WTACC[:, ch * 512:(ch + 1) * 512],
                                               in0=UVU[:, cs * 512:(cs + 1) * 512],
                                               in1=UVV[:, cs * 512:(cs + 1) * 512], op=OP.mult)
                    else:
                        bk.chain(v, v.tensor_tensor(out=SCR1[:, :],
                                                     in0=UVU[:, cs * 512:(cs + 1) * 512],
                                                     in1=UVV[:, cs * 512:(cs + 1) * 512],
                                                     op=OP.mult), "sV")
                        inst = v.tensor_tensor(out=WTACC[:, ch * 512:(ch + 1) * 512],
                                               in0=WTACC[:, ch * 512:(ch + 1) * 512],
                                               in1=SCR1[:, :], op=OP.add)
                    nv_ = bk.inc(inst, "sV", 1)
                    EV["r12_mul"] = nv_
                    EV["r12_mul_m2"] = EV.get("r12_mul_prev", 0)
                    EV["r12_mul_prev"] = nv_
                VE(mul)

        def cast(v, which=which):
            bk.w(v, "sV")
            for ch in range(4):
                src = WTACC[:, ch * 512:(ch + 1) * 512]
                if which == 0:
                    bk.inc(v.tensor_copy(out=R12B[:, ch * 512:(ch + 1) * 512], in_=src), "sV", 1)
                    bk.inc(v.tensor_copy(
                        out=R12B[:, 4096 + (4 + ch) * 512: 4096 + (5 + ch) * 512], in_=src),
                        "sV", 1)
                else:
                    bk.inc(v.tensor_copy(out=R12B[:, 4096 + ch * 512: 4096 + (ch + 1) * 512], in_=src), "sV", 1)
                    bk.inc(v.tensor_scalar(out=R12B[:, (4 + ch) * 512:(5 + ch) * 512],
                                           in0=src, scalar1=-1.0, scalar2=None, op0=OP.mult),
                           "sV", 1)
            EV["r12_cast"] = bk.v("sV")
        VE(cast)

    def r12_spill(sy):
        bk.w(sy, "sV", EV["r12_cast"])
        r3 = r12sp.ap().rearrange("r c p t -> p r c t")
        r12bv = R12B[:, :].rearrange("p (r c t) -> p r c t", r=2, t=512)
        bk.inc(sy.dma_start(out=r3, in_=r12bv), "d_rsp", 16)
        EV["r12_sp"] = bk.v("d_rsp")
    SY(r12_spill)

    def mbig_load(sy):
        bk.w(sy, "d_rsp", EV["r12_sp"])
        bk.inc(sy.dma_start(out=MBIG[:, :], in_=mbig), "d_mb", 16)
        EV["mbig0"] = bk.v("d_mb")
    SY(mbig_load)

    def attention(layer, dst_buf, rrep_mode, qs_end=None):
        tagm = "l0" if layer == 0 else "l1"

        exp_ev = {}
        avd_ev = {}
        sc_ev = {}
        msk_ev = {}

        def avd_mm(t, hp, kbd, gd, nkb):
            bk.w(t, "sS", exp_ev[gd])
            if gd in msk_ev:
                bk.w(t, "sV", msk_ev[gd])
            if kbd == 0:
                if "att_evac" in EV:
                    bk.w(t, "sV", EV["att_evac"])
                if "att_lnd" in EV:
                    bk.w(t, "sS", EV["att_lnd"])
                if f"vones{layer}" in EV:
                    bk.w(t, "sV", EV[f"vones{layer}"])
            ps = gd % 8
            t.matmul(ps_av[0:65, :], vsl(kbd, hp, 0),
                     PT[:, ps * 1024: ps * 1024 + 512],
                     start=(kbd == 0), stop=(kbd == nkb - 1))
            inst = t.matmul(ps_den[0:65, :], vsl(kbd, hp, 1),
                            PT[:, ps * 1024 + 512:(ps + 1) * 1024],
                            start=(kbd == 0), stop=(kbd == nkb - 1))
            bk.inc(inst, "sT", 1)
            avd_ev[gd] = bk.v("sT")

        for hp in range(4):
            gkb = 0
            for qs in range(8):
                nkb = 4 * qs + 4

                def emit_scores(k, t, hp=hp, qs=qs):
                    kb = k
                    slot = gkb_of[k] % 2
                    scb = ps_sc0 if slot == 0 else ps_sc1
                    if gkb_of[k] - 2 in gexp:
                        bk.w(t, "sS", gexp[gkb_of[k] - 2])
                    t.matmul(scb[:, 0:512],
                             KFM[0:64, hp * 4096 + kb * 128: hp * 4096 + (kb + 1) * 128],
                             QFM[0:64, hp * 4096 + qs * 512: hp * 4096 + (qs + 1) * 512],
                             start=True, stop=True, tile_position=(0, 0))
                    inst = t.matmul(scb[:, 512:1024],
                                    KFM[64:128, hp * 4096 + kb * 128: hp * 4096 + (kb + 1) * 128],
                                    QFM[64:128, hp * 4096 + qs * 512: hp * 4096 + (qs + 1) * 512],
                                    start=True, stop=True, tile_position=(64, 0))
                    bk.inc(inst, "sT", 1)
                    gsc[gkb_of[k]] = bk.v("sT")

                pass

                def mk_scores(kb, g, scb, hp=hp, qs=qs):
                    def t_scores(t, hp=hp, qs=qs, kb=kb, g=g, scb=scb):
                        if g == 0 and hp == 0:
                            bk.w(t, "sG", EV[f"biasm_{tagm}_3"])
                            bk.w(t, "d_mb" if layer == 0 else "d_mb2",
                                 EV["mbig0" if layer == 0 else "mbig1"])
                        if g - 2 in exp_ev:
                            bk.w(t, "sS", exp_ev[g - 2])
                        t.matmul(scb[:, 0:512],
                                 KFM[0:64, hp * 4096 + kb * 128: hp * 4096 + (kb + 1) * 128],
                                 QFM[0:64, hp * 4096 + qs * 512: hp * 4096 + (qs + 1) * 512],
                                 start=True, stop=True, tile_position=(0, 0))
                        inst = t.matmul(scb[:, 512:1024],
                                        KFM[64:128, hp * 4096 + kb * 128: hp * 4096 + (kb + 1) * 128],
                                        QFM[64:128, hp * 4096 + qs * 512: hp * 4096 + (qs + 1) * 512],
                                        start=True, stop=True, tile_position=(64, 0))
                        bk.inc(inst, "sT", 1)
                        sc_ev[g] = bk.v("sT")
                    return t_scores

                for kb in range(nkb):
                    g = gkb + kb
                    diag = kb >= 4 * qs
                    j = kb - 4 * qs
                    slot = g % 2
                    scb = ps_sc0 if slot == 0 else ps_sc1
                    pslot = g % 8

                    if kb % 2 == 0:
                        TE(mk_scores(kb, g, scb))
                        TE(mk_scores(kb + 1, g + 1,
                                     ps_sc0 if (g + 1) % 2 == 0 else ps_sc1))

                    def s_exp(s, hp=hp, g=g, scb=scb, pslot=pslot):
                        bk.w(s, "sT", sc_ev[g])
                        if g - 8 in avd_ev:
                            bk.w(s, "sT", avd_ev[g - 8])
                        inst = s.activation(out=PT[:, pslot * 1024:(pslot + 1) * 1024],
                                            in_=scb[:, :], func=AF.Exp,
                                            bias=BIASM[:, hp:hp + 1], scale=0.125)
                        bk.inc(inst, "sS", 1)
                        exp_ev[g] = bk.v("sS")
                    SE(s_exp)

                    if diag:
                        def v_mask(v, g=g, j=j, pslot=pslot):
                            bk.w(v, "sS", exp_ev[g])
                            moff = 384 - 128 * j
                            bk.inc(v.tensor_tensor(
                                out=PT[:, pslot * 1024: pslot * 1024 + 512],
                                in0=PT[:, pslot * 1024: pslot * 1024 + 512],
                                in1=MBIG[:, moff:moff + 512], op=OP.mult), "sV", 1)
                            inst = v.tensor_tensor(
                                out=PT[:, pslot * 1024 + 512:(pslot + 1) * 1024],
                                in0=PT[:, pslot * 1024 + 512:(pslot + 1) * 1024],
                                in1=MBIG[:, moff:moff + 512], op=OP.mult)
                            bk.inc(inst, "sV", 1)
                            msk_ev[g] = bk.v("sV")
                        VE(v_mask)

                    gd = g - 2
                    if gd >= gkb:
                        TE(lambda t, hp=hp, kbd=gd - gkb, gd=gd, nkb=nkb:
                           avd_mm(t, hp, kbd, gd, nkb))

                for gd in (gkb + nkb - 2, gkb + nkb - 1):
                    TE(lambda t, hp=hp, kbd=gd - gkb, gd=gd, nkb=nkb:
                       avd_mm(t, hp, kbd, gd, nkb))
                EV["att_av"] = bk.v("sT")

                if rrep_mode:
                    def s_den(s, hp=hp, qs=qs):
                        bk.w(s, "sT", EV["att_av"])
                        bk.chain(s, s.activation(out=S1D[64:65, :], in_=ps_avd[64:65, :],
                                                 func=AF.Ln), "sS")
                        inst = s.activation(out=RDD[64:65, 0:1024], in_=S1D[64:65, :],
                                            func=AF.Exp, scale=-1.0)
                        bk.inc(inst, "sS", 1)
                        EV["att_lnd"] = bk.v("sS")
                    SE(s_den)

                    def t_rep(t, hp=hp, qs=qs):
                        bk.w(t, "sS", EV["att_lnd"])
                        t.matmul(ps_m1[0:64, 0:512], ONESR[64:65, 0:64],
                                 RDD[64:65, 0:512], start=True, stop=True,
                                 tile_position=(64, 0), skip_group_check=True)
                        inst = t.matmul(ps_m1[64:128, 0:512], ONESR[64:65, 0:64],
                                        RDD[64:65, 512:1024], start=True, stop=True,
                                        tile_position=(64, 64), skip_group_check=True)
                        bk.inc(inst, "sT", 1)
                        EV["att_rep"] = bk.v("sT")
                    TE(t_rep)

                    def v_evac(v, hp=hp, qs=qs):
                        bk.w(v, "sT", EV["att_rep"])
                        bk.chain(v, v.tensor_copy(out=RREPS[:, :], in_=ps_m1[:, 0:512]), "sV")
                        cs = hp * 4096 + qs * 512
                        bk.inc(v.tensor_tensor(
                            out=dst_buf[0:64, cs:cs + 512],
                            in0=ps_av[0:64, :], in1=RREPS[0:64, :], op=OP.mult), "sV", 1)
                        inst = v.tensor_tensor(
                            out=dst_buf[64:128, cs:cs + 512],
                            in0=ps_den[0:64, :], in1=RREPS[64:128, :], op=OP.mult)
                        bk.inc(inst, "sV", 1)
                        EV["att_evac"] = bk.v("sV")
                    VE(v_evac)
                else:
                    def v_den(v, hp=hp, qs=qs):
                        bk.w(v, "sT", EV["att_av"])
                        inst = v.tensor_copy(out=RDD[64:65, 0:1024],
                                             in_=ps_avd[64:65, :])
                        bk.inc(inst, "sV", 1)
                        EV["att_dencp"] = bk.v("sV")
                    VE(v_den)

                    def t_rdt(t, hp=hp, qs=qs):
                        bk.w(t, "sV", EV["att_dencp"])
                        for qb in range(4):
                            t.transpose(ps_m1[:, qb:qb + 1],
                                        RDD[64:65, qb * 128:(qb + 1) * 128],
                                        ONESF[64:65, 0:1])
                        for qb in range(4):
                            inst = t.transpose(ps_m1[:, 4 + qb:5 + qb],
                                               RDD[64:65, 512 + qb * 128: 512 + (qb + 1) * 128],
                                               ONESF[64:65, 0:1])
                        bk.inc(inst, "sT", 1)
                        EV["att_rdt"] = bk.v("sT")
                    TE(t_rdt)

                    def v_evac(v, hp=hp, qs=qs):
                        bk.w(v, "sT", EV["att_rdt"])
                        bk.inc(v.tensor_copy(out=RDT[:, hp * 64 + qs * 8: hp * 64 + qs * 8 + 8],
                                             in_=ps_m1[:, 0:8]), "sV", 1)
                        cs = hp * 4096 + qs * 512
                        bk.inc(v.tensor_copy(out=dst_buf[0:64, cs:cs + 512],
                                             in_=ps_av[0:64, :]), "sV", 1)
                        inst = v.tensor_copy(out=dst_buf[64:128, cs:cs + 512],
                                             in_=ps_den[0:64, :])
                        bk.inc(inst, "sV", 1)
                        EV["att_evac"] = bk.v("sV")
                    VE(v_evac)
                gkb += nkb
                if qs_end is not None:
                    qs_end(hp, qs)
        EV[f"attn{layer}_done"] = bk.v("sV")

    def scr_rcp(c):
        def s_rcp(s, c=c):
            bk.w(s, "sV", EV["att_evac"])
            bk.chain(s, s.activation(out=RDTL[:, c * 64:(c + 1) * 64],
                                     in_=RDT[:, c * 64:(c + 1) * 64], func=AF.Ln), "sS")
            inst = s.activation(out=RDT[:, c * 64:(c + 1) * 64],
                                in_=RDTL[:, c * 64:(c + 1) * 64], func=AF.Exp, scale=-1.0)
            bk.inc(inst, "sS", 1)
            EV[f"rdt_rcp_{c}"] = bk.v("sS")
        SE(s_rcp)

    def scr_group(c, g):
        def tt_(t, c=c, g=g):
            if g == 0:
                bk.w(t, "sV", EV[f"scr_cat_{c}"])
            if "scr_ev" in EV:
                bk.w(t, "sV", EV["scr_ev"])
            for wi in range(4):
                w = g * 4 + wi
                inst = t.transpose(ps_tb[:, wi * 128:(wi + 1) * 128],
                                   CAT[:, c * 4096 + w * 128: c * 4096 + (w + 1) * 128],
                                   IDB[:, :])
            bk.inc(inst, "sT", 1)
            EV["scr_tr"] = bk.v("sT")
        TE(tt_)

        def vv_(v, c=c, g=g):
            bk.w(v, "sT", EV["scr_tr"])
            bk.w(v, "sS", EV[f"rdt_rcp_{c}"])
            for wi in range(4):
                w = g * 4 + wi
                ec = w % 8
                sq = w // 8
                base = c * 4096 + ec * 512 + sq
                rcol = c * 64 + (w // 4) * 8
                bk.inc(v.tensor_scalar(out=SCRM[:, base: base + 253: 4],
                                       in0=ps_tb[:, wi * 128:wi * 128 + 64],
                                       scalar1=RDT[:, rcol + wi:rcol + wi + 1],
                                       scalar2=None, op0=OP.mult), "sV", 1)
                bk.inc(v.tensor_scalar(out=SCRM[:, base + 256: base + 509: 4],
                                       in0=ps_tb[:, wi * 128 + 64:(wi + 1) * 128],
                                       scalar1=RDT[:, rcol + 4 + wi:rcol + 5 + wi],
                                       scalar2=None, op0=OP.mult), "sV", 1)
            EV["scr_ev"] = bk.v("sV")
        VE(vv_)

    def scr_hook(hp, qs):
        if qs == 7:
            EV[f"scr_cat_{hp}"] = EV["att_evac"]
            scr_rcp(hp)
        if hp >= 1:
            scr_group(hp - 1, qs)

    attention(0, CAT, rrep_mode=False, qs_end=scr_hook)
    for g in range(8):
        scr_group(3, g)
    EV["p3_scr"] = bk.v("sV")

    if DBG:
        dump(dbg_cat, CAT[:, :])
        dump(dbg_rdt, RDT[:, :])

    def p3_wload(g):
        bk.w(g, "sV", EV["attn0_done"])
        inst = g.iota(ONESI[:, :], pattern=[[0, 512]], base=1, channel_multiplier=0)
        bk.inc(inst, "sG", 1)
        EV["onesi"] = bk.v("sG")
        bk.inc(g.dma_start(out=sb3(WOB[:, :], 1024), in_=fm3(wo0)), "d_wo", 16)
        EV["p3_wo"] = bk.v("d_wo")
    GE(p3_wload)

    def p3_wfload(sy):
        bk.w(sy, "sV", EV["attn0_done"])
        bk.inc(sy.dma_start(out=sb3(WF32[:, :], 1024), in_=fm3(wf0)), "d_wf", 16)
        EV["p3_wf"] = bk.v("d_wf")
    SY(p3_wfload)


    if DBG:
        dump(dbg_scrm, SCRM[:, :])

    def wam_v1(v):
        bk.w(v, "d_wf", EV["p3_wf"])
        for c in range(7):
            bk.inc(v.reduce_max(out=AMXC[:, c:c + 1], in_=WF32[:, c * 1024:(c + 1) * 1024],
                         axis=AX.X, apply_absolute_value=True), "sV", 1)
        bk.chain(v, v.reduce_max(out=AMXC[:, 7:8], in_=WF32[:, 7 * 1024:8 * 1024],
                                 axis=AX.X, apply_absolute_value=True), "sV")
        inst = v.reduce_max(out=AMXC[:, 8:9], in_=AMXC[:, 0:8], axis=AX.X)
        bk.inc(inst, "sV", 1)
        EV["wam_v"] = bk.v("sV")
    VE(wam_v1)

    def wam_t1(t):
        bk.w(t, "sV", EV["wam_v"])
        inst = t.transpose(ps_m1[0:1, 128:256], AMXC[:, 8:9], IDF[:, :])
        bk.inc(inst, "sT", 1)
        EV["wam_t"] = bk.v("sT")
    TE(wam_t1)

    def wam_v2(v):
        bk.w(v, "sT", EV["wam_t"])
        bk.chain(v, v.reduce_max(out=SC1[0:1, 0:1], in_=ps_m1[0:1, 128:256], axis=AX.X), "sV")
        bk.chain(v, v.tensor_scalar(out=SC1[0:1, 1:2], in0=SC1[0:1, 0:1], scalar1=1e-12,
                                    scalar2=None, op0=OP.max), "sV")
        bk.chain(v, v.reciprocal(out=SC1[0:1, 2:3], in_=SC1[0:1, 1:2]), "sV")
        inst = v.tensor_scalar(out=SC1[0:1, 3:4], in0=SC1[0:1, 2:3], scalar1=448.0,
                               scalar2=None, op0=OP.mult)
        bk.inc(inst, "sV", 1)
        EV["wam_s"] = bk.v("sV")
    VE(wam_v2)

    def wam_g1(g):
        bk.w(g, "sV", EV["wam_s"])
        inst = g.partition_broadcast(SCWV[:, :], SC1[0:1, 3:4])
        bk.inc(inst, "sG", 1)
        EV["wam_b"] = bk.v("sG")
    GE(wam_g1)

    def quant_tile(src, dst, sv, waits, alt=False):
        sid = 1 if alt else 0
        qa, qz, qe, qsg = (QA1, QZ1, QE1, QSG1) if alt else (QA, QZ, QE, QSG)

        def s1(s):
            for wn, wv_ in waits:
                bk.w(s, wn, wv_)
            if f"q_v4_{sid}" in EV:
                bk.w(s, "sV", EV[f"q_v4_{sid}"])
            bk.chain(s, s.activation(out=qa[:, :], in_=src, func=AF.Abs,
                                     scale=sv[:, 0:1]), "sS")
            bk.chain(s, s.activation(out=qz[:, :], in_=qa[:, :], func=AF.Ln,
                                     bias=CST[:, 0:1]), "sS")
            bk.inc(s.activation(out=qe[:, :], in_=qz[:, :], func=AF.Identity,
                                scale=INVLN2, bias=CST[:, 1:2]), "sS", 1)
            bk.inc(s.activation(out=qsg[:, :], in_=src, func=AF.Sign), "sS", 1)
            EV["q_s1"] = bk.v("sS")
        SE(s1)

        def v1(v):
            bk.w(v, "sS", EV["q_s1"])
            bk.chain(v, v.scalar_tensor_tensor(out=qz[:, :], in0=qz[:, :], scalar=INVLN2,
                                               in1=qe[:, :], op0=OP.mult, op1=OP.subtract), "sV")
            bk.chain(v, v.tensor_scalar(out=QTC[:, :], in0=qz[:, :], scalar1=-16.0,
                                        scalar2=None, op0=OP.is_lt), "sV")
            bk.chain(v, v.tensor_tensor(out=qe[:, :], in0=qe[:, :], in1=QTC[:, :],
                                        op=OP.subtract), "sV")
            bk.chain(v, v.tensor_scalar(out=QT[:, :], in0=qe[:, :], scalar1=-10.0,
                                        scalar2=0.0, op0=OP.add, op1=OP.max), "sV")
            bk.chain(v, v.tensor_scalar(out=qe[:, :], in0=QT[:, :], scalar1=-1.0,
                                        scalar2=15.0, op0=OP.mult, op1=OP.add), "sV")
            bk.chain(v, v.tensor_tensor(out=QM[:, :], in0=ONESI[:, :], in1=QT[:, :],
                                        op=OP.logical_shift_left), "sV")
            bk.chain(v, v.tensor_tensor(out=QT[:, :], in0=ONESI[:, :], in1=qe[:, :],
                                        op=OP.logical_shift_left), "sV")
            bk.chain(v, v.scalar_tensor_tensor(out=qz[:, :], in0=qa[:, :],
                                               scalar=float(2.0 ** -6), in1=QT[:, :],
                                               op0=OP.mult, op1=OP.mult), "sV")
            bk.chain(v, v.tensor_scalar(out=qe[:, :], in0=qz[:, :], scalar1=1.0,
                                        scalar2=None, op0=OP.mult), "sV")
            bk.chain(v, v.scalar_tensor_tensor(out=QTC[:, :], in0=qz[:, :], scalar=-0.5,
                                               in1=qe[:, :], op0=OP.add, op1=OP.is_gt), "sV")
            bk.chain(v, v.tensor_tensor(out=qe[:, :], in0=qe[:, :], in1=QTC[:, :],
                                        op=OP.add), "sV")
            bk.chain(v, v.scalar_tensor_tensor(out=qz[:, :], in0=qe[:, :],
                                               scalar=float(2.0 ** -9), in1=QM[:, :],
                                               op0=OP.mult, op1=OP.mult), "sV")
            bk.inc(v.tensor_tensor(out=dst, in0=qz[:, :], in1=qsg[:, :], op=OP.mult),
                   "sV", 1)
            EV[f"q_v4_{sid}"] = bk.v("sV")
        VE(v1)
        return bk.v("sV")


    for st in range(4):
        slot = st % 2

        def p4_x0o(sy, st=st, slot=slot):
            if st < 2:
                bk.w(sy, "sV", EV["p3_scr"])
            else:
                bk.w(sy, "sV", EV[f"p4_x1_{st-2}"])
            sem = f"dx{slot}"
            bk.inc(sy.dma_start(out=sb3(X0OR[:, slot * 4096:(slot + 1) * 4096], 512),
                                in_=fm3(x0o[:, st * 512:(st + 1) * 512])), sem, 16)
            EV[f"p4_x0o_{st}"] = (sem, bk.v(sem))
        SY(p4_x0o)

        for oc in range(8):
            half = (st * 8 + oc) % 2
            pb = ps_sc0 if half == 0 else ps_sc1

            def p4_mm(t, st=st, oc=oc, pb=pb, half=half):
                if st == 0 and oc == 0:
                    bk.w(t, "sV", EV["p3_scr"])
                    bk.w(t, "d_wo", EV["p3_wo"])
                if f"p4_ev_h{half}" in EV:
                    bk.w(t, "sV", EV[f"p4_ev_h{half}"])
                for ec in range(8):
                    inst = t.matmul(pb[:, 0:512],
                                    WOB[:, ec * 1024 + oc * 128: ec * 1024 + (oc + 1) * 128],
                                    SCRM[:, st * 4096 + ec * 512: st * 4096 + (ec + 1) * 512],
                                    start=(ec == 0), stop=(ec == 7))
                bk.inc(inst, "sT", 1)
                EV["p4_mm"] = bk.v("sT")
            TE(p4_mm)

            def p4_ev(v, st=st, oc=oc, slot=slot, pb=pb, half=half):
                bk.w(v, "sT", EV["p4_mm"])
                bk.w(v, EV[f"p4_x0o_{st}"][0], EV[f"p4_x0o_{st}"][1])
                if st == 0 and oc == 0:
                    bk.w(v, "d_rsp", EV["r12_sp"])
                inst = v.tensor_tensor(
                    out=STRM[:, oc * 2048 + st * 512: oc * 2048 + (st + 1) * 512],
                    in0=pb[:, 0:512],
                    in1=X0OR[:, slot * 4096 + oc * 512: slot * 4096 + (oc + 1) * 512],
                    op=OP.add)
                bk.inc(inst, "sV", 1)
                EV[f"p4_ev_h{half}"] = bk.v("sV")
            VE(p4_ev)

            k4 = st * 8 + oc
            if k4 % 2 == 1:
                ti = k4 // 2
                quant_tile(WF32[:, ti * 512:(ti + 1) * 512],
                           WQT[:, ti * 512:(ti + 1) * 512],
                           SCWV, [("sG", EV["wam_b"]), ("sG", EV["onesi"])])
        EV[f"p4_x1_{st}"] = bk.v("sV")
    EV["p4_done"] = bk.v("sV")

    if DBG:
        dump(dbg_x1, STRM[:, :])
        dump(dbg_wqt, WQT[:, :])

    last_sp = {0: None, 1: None}
    for st in range(4):
        def ssrc(c, st=st):
            return STRM[:, c * 2048 + st * 512: c * 2048 + (st + 1) * 512]
        ln_stats("ln1", st, ssrc, ("sV", EV["p4_done"]))

        for c in range(8):
            cs = c % 2

            def nv(v, st=st, c=c, cs=cs):
                bk.w(v, "sT", EV[f"ln1_rep_{st}"])
                if last_sp[cs] is not None:
                    bk.w(v, last_sp[cs][0], last_sp[cs][1])
                bk.chain(v, v.tensor_tensor(out=SCR1[:, :], in0=ssrc(c),
                                             in1=ps_sc0[:, 0:512], op=OP.mult), "sV")
                bk.chain(v, v.tensor_tensor(out=LN1T[:, cs * 512:(cs + 1) * 512],
                                            in0=SCR1[:, :], in1=ps_sc0[:, 512:1024],
                                            op=OP.subtract), "sV")
                inst = v.reduce_max(out=AMXC[:, st * 8 + c: st * 8 + c + 1],
                                    in_=LN1T[:, cs * 512:(cs + 1) * 512], axis=AX.X,
                                    apply_absolute_value=True)
                bk.inc(inst, "sV", 1)
                EV[f"ln1_t_{st}_{c}"] = bk.v("sV")
            VE(nv)

            def nd(sy, st=st, c=c, cs=cs):
                bk.w(sy, "sV", EV[f"ln1_t_{st}_{c}"])
                sem = f"d_sp{cs}"
                bk.inc(sy.dma_start(out=ln1sp.ap()[c][:, st * 512:(st + 1) * 512],
                                    in_=LN1T[:, cs * 512:(cs + 1) * 512]), sem, 16)
                EV[f"ln1_dma_{st}_{c}"] = (sem, bk.v(sem))
                last_sp[cs] = (sem, bk.v(sem))
            SY(nd)
    EV["ln1_alldma"] = (bk.v("d_sp0"), bk.v("d_sp1"))
    EV["ln1_allv"] = bk.v("sV")

    def am_v1(v):
        bk.w(v, "sV", EV["ln1_allv"])
        inst = v.reduce_max(out=SCR1[:, 0:1], in_=AMXC[:, 0:32], axis=AX.X)
        bk.inc(inst, "sV", 1)
        EV["am_v"] = bk.v("sV")
    VE(am_v1)

    def am_t1(t):
        bk.w(t, "sV", EV["am_v"])
        inst = t.transpose(ps_m1[0:1, 128:256], SCR1[:, 0:1], IDF[:, :])
        bk.inc(inst, "sT", 1)
        EV["am_t"] = bk.v("sT")
    TE(am_t1)

    def am_v2(v):
        bk.w(v, "sT", EV["am_t"])
        bk.chain(v, v.memset(SCAL[0:1, :], 0.0), "sV")
        inst = v.reduce_max(out=SCAL[0:1, 0:1], in_=ps_m1[0:1, 128:256], axis=AX.X)
        bk.inc(inst, "sV", 1)
        EV["am_l"] = bk.v("sV")
    VE(am_v2)

    def am_g1(g):
        bk.w(g, "sV", EV["am_l"])
        bk.inc(g.dma_start(out=am_in.ap(), in_=SCAL[0:1, :]), "d_am", 16)
        bk.w(g, "d_am")
        inst = g.collective_compute("AllReduce", OP.max,
                                    replica_groups=[list(range(NCORES))],
                                    ins=[am_in.ap().opt()], outs=[am_out.ap().opt()])
        bk.inc(inst, "cc", 1)
        bk.w(g, "cc")
        bk.inc(g.dma_start(out=SCAL[0:1, :], in_=am_out.ap()), "d_am", 16)
        bk.w(g, "d_am")
        bk.inc(g.nop(), "sG", 1)
        EV["am_g"] = bk.v("sG")
    GE(am_g1)

    def am_v3(v):
        bk.w(v, "sG", EV["am_g"])
        bk.chain(v, v.tensor_scalar(out=SC1[0:1, 4:5], in0=SCAL[0:1, 0:1], scalar1=1e-12,
                                     scalar2=None, op0=OP.max), "sV")
        bk.chain(v, v.reciprocal(out=SC1[0:1, 5:6], in_=SC1[0:1, 4:5]), "sV")
        bk.chain(v, v.tensor_scalar(out=SC1[0:1, 4:5], in0=SC1[0:1, 5:6], scalar1=448.0,
                                    scalar2=None, op0=OP.mult), "sV")
        bk.chain(v, v.reciprocal(out=SC1[0:1, 6:7], in_=SC1[0:1, 4:5]), "sV")
        bk.chain(v, v.reciprocal(out=SC1[0:1, 7:8], in_=SC1[0:1, 3:4]), "sV")
        inst = v.tensor_tensor(out=SC1[0:1, 8:9], in0=SC1[0:1, 6:7], in1=SC1[0:1, 7:8],
                               op=OP.mult)
        bk.inc(inst, "sV", 1)
        EV["am_s"] = bk.v("sV")
    VE(am_v3)

    def am_g2(g):
        bk.w(g, "sV", EV["am_s"])
        bk.inc(g.partition_broadcast(SCXV[:, :], SC1[0:1, 4:5]), "sG", 1)
        inst = g.partition_broadcast(DSCV[:, :], SC1[0:1, 8:9])
        bk.inc(inst, "sG", 1)
        EV["am_b"] = bk.v("sG")
    GE(am_g2)

    if DBG:
        def _dsc(v):
            bk.w(v, "sG", EV["am_b"])
            bk.inc(v.tensor_copy(out=SCR1[:, 0:1], in_=SCXV[:, :]), "sV", 1)
            bk.inc(v.tensor_copy(out=SCR1[:, 1:2], in_=SCWV[:, :]), "sV", 1)
            bk.inc(v.tensor_copy(out=SCR1[:, 2:3], in_=DSCV[:, :]), "sV", 1)
            bk.inc(v.tensor_copy(out=SCR1[0:1, 3:4], in_=SCAL[0:1, 0:1]), "sV", 1)
        VE(_dsc)
        dump(dbg_sc, SCR1[:, 0:4])

    for st in range(4):
        slot = st % 2
        for c in range(8):
            cs = c % 2

            def p5_ld(sy, st=st, c=c, cs=cs):
                bk.w(sy, "d_sp0", EV["ln1_alldma"][0])
                bk.w(sy, "d_sp1", EV["ln1_alldma"][1])
                ti = st * 8 + c
                if f"p5_qt_{ti-2}" in EV:
                    bk.w(sy, "sV", EV[f"p5_qt_{ti-2}"])
                if f"p6_mm_{st-1}" in EV and c == 0:
                    bk.w(sy, "sT", EV[f"p6_mm_{st-1}"])
                sem = f"dx{cs}"
                bk.inc(sy.dma_start(out=LN1R[:, cs * 512:(cs + 1) * 512],
                                    in_=ln1sp.ap()[c][:, st * 512:(st + 1) * 512]), sem, 16)
                EV[f"p5_ld_{st}_{c}"] = (sem, bk.v(sem))
            SY(p5_ld)

            ti = st * 8 + c
            xw = [EV[f"p5_ld_{st}_{c}"], ("sG", EV["am_b"])]
            if ti % 2 == 1:
                xw.append(("sV", EV["p3_scr"]))
            EV[f"p5_qt_{ti}"] = quant_tile(
                LN1R[:, cs * 512:(cs + 1) * 512],
                XQR[:, slot * 4096 + c * 512: slot * 4096 + (c + 1) * 512],
                SCXV, xw, alt=(ti % 2 == 1))
        EV[f"p5_q_{st}"] = bk.v("sV")

        for oc in range(8):
            half = oc % 2
            pb = ps_sc0 if half == 0 else ps_sc1

            def p6_mm(t, st=st, oc=oc, slot=slot, pb=pb, half=half):
                bk.w(t, "sV", EV[f"p5_q_{st}"])
                if f"p6_ev_h{half}" in EV:
                    bk.w(t, "sV", EV[f"p6_ev_h{half}"])
                for ic in range(8):
                    inst = t.matmul(pb[:, 0:512],
                                    WQT[:, ic * 1024 + oc * 128: ic * 1024 + oc * 128 + 128],
                                    XQR[:, slot * 4096 + ic * 512: slot * 4096 + (ic + 1) * 512],
                                    start=(ic == 0), stop=(ic == 7))
                bk.inc(inst, "sT", 1)
                EV["p6_mmx"] = bk.v("sT")
            TE(p6_mm)

            def p6_ev(v, st=st, oc=oc, pb=pb, half=half):
                bk.w(v, "sT", EV["p6_mmx"])
                bk.chain(v, v.tensor_scalar(out=SCR1[:, :], in0=pb[:, 0:512],
                                             scalar1=DSCV[:, 0:1], scalar2=BF0[:, oc:oc + 1],
                                             op0=OP.mult, op1=OP.add), "sV")
                inst = v.tensor_tensor(
                    out=STRM[:, oc * 2048 + st * 512: oc * 2048 + (st + 1) * 512],
                    in0=SCR1[:, :],
                    in1=STRM[:, oc * 2048 + st * 512: oc * 2048 + (st + 1) * 512],
                    op=OP.add)
                bk.inc(inst, "sV", 1)
                EV[f"p6_ev_h{half}"] = bk.v("sV")
            VE(p6_ev)
        EV[f"p6_mm_{st}"] = bk.v("sT")
    EV["p6_done"] = bk.v("sV")

    if DBG:
        dump(dbg_x2, STRM[:, :])
        dump(dbg_xq, XQR[:, :])

        def _dln1(sy):
            bk.inc(sy.dma_start(out=dbg_ln1, in_=ln1sp.ap()), "d0", 16)
            bk.w(sy, "d0")
        SY(_dln1)

    for st in range(4):
        def ssrc2(c, st=st):
            return STRM[:, c * 2048 + st * 512: c * 2048 + (st + 1) * 512]
        ln_stats("ln2", st, ssrc2, ("sV", EV["p6_done"]))

        def nv2(v, st=st):
            bk.w(v, "sT", EV[f"ln2_rep_{st}"])
            for c in range(8):
                bk.w(v, "sV")
                bk.chain(v, v.tensor_tensor(out=SCR1[:, :], in0=ssrc2(c),
                                             in1=ps_sc0[:, 0:512], op=OP.mult), "sV")
                bk.inc(v.tensor_tensor(
                    out=LN2OWN[:, c * 2048 + st * 512: c * 2048 + (st + 1) * 512],
                    in0=SCR1[:, :], in1=ps_sc0[:, 512:1024], op=OP.subtract), "sV", 1)
            EV[f"ln2own_{st}"] = bk.v("sV")
        VE(nv2)

        def p7_send(g, st=st):
            bk.w(g, "sV", EV[f"ln2own_{st}"])
            inst = g.dma_start(
                out=fm3(agl_in.ap())[:, :, st * 512:(st + 1) * 512],
                in_=sb3(LN2OWN[:, :], 2048)[:, :, st * 512:(st + 1) * 512])
            bk.inc(inst, "d_am", 16)
        GE(p7_send)
    EV["ln2_done"] = bk.v("sV")

    def p7_exchange(g):
        bk.w(g, "d_am")
        inst = g.collective_compute("AllGather", OP.bypass, replica_groups=PAIRS,
                                    ins=[agl_in.ap().opt()], outs=[agl_out.ap().opt()])
        bk.inc(inst, "cc", 1)
        EV["p7_cc"] = bk.v("cc")
    GE(p7_exchange)

    def p7_spill(sy):
        bk.w(sy, "sV", EV["p6_done"])
        bk.inc(sy.dma_start(out=fm3(x2sp.ap()), in_=sb3(STRM[:, :], 2048)), "d_xsp", 16)
        EV["p7_sp"] = bk.v("d_xsp")
    SY(p7_spill)

    def p8_wload(g):
        bk.w(g, "sV", EV["p6_done"])
        bk.w(g, "d_xsp", EV["p7_sp"])
        bk.inc(g.dma_start(out=MBIG[:, :], in_=mbig), "d_mb2", 16)
        EV["mbig1"] = bk.v("d_mb2")
        for i, w_ in enumerate([wq1, wk1, wv1]):
            bk.inc(g.dma_start(out=sb3(WQKV1[:, i * 4096:(i + 1) * 4096], 512),
                               in_=fm3(w_)), "d_wk", 16)
        EV["p8_w"] = bk.v("d_wk")
    GE(p8_wload)

    def v_ones1(v):
        bk.w(v, "sT", bk.v("sT"))
        inst = v.memset(VTM[:, :].rearrange("p (c t) -> p c t", t=65)[:, :, 64:65], 1.0)
        bk.inc(inst, "sV", 1)
        EV["vones1"] = bk.v("sV")
    VE(v_ones1)

    for gqs in range(8):
        slot = gqs % 2

        def p8_ld(sy, gqs=gqs, slot=slot):
            bk.w(sy, "cc", EV["p7_cc"])
            bk.w(sy, "d_xsp", EV["p7_sp"])
            if f"p8_qkv_{gqs-2}" in EV:
                bk.w(sy, "sV", EV[f"p8_qkv_{gqs-2}"])
            blockr = gqs // 4
            coff = (gqs % 4) * 512
            srcv = fm3(agl_out.ap()[blockr * 1024:(blockr + 1) * 1024, coff:coff + 512])
            sem = f"dx{slot}"
            bk.inc(sy.dma_start(out=sb3(LN2FR[:, slot * 4096:(slot + 1) * 4096], 512),
                                in_=srcv), sem, 16)
            EV[f"p8_ld_{gqs}"] = (sem, bk.v(sem))
        SY(p8_ld)

        for hp in range(4):
            def p8_qk(t, gqs=gqs, hp=hp, slot=slot):
                bk.w(t, EV[f"p8_ld_{gqs}"][0], EV[f"p8_ld_{gqs}"][1])
                bk.w(t, "d_wk", EV["p8_w"])
                if "p8_qk_ev" in EV:
                    bk.w(t, "sV", EV["p8_qk_ev"])
                for fc in range(8):
                    t.matmul(ps_sc0[:, 0:512],
                             WQKV1[:, fc * 512 + hp * 128: fc * 512 + (hp + 1) * 128],
                             LN2FR[:, slot * 4096 + fc * 512: slot * 4096 + (fc + 1) * 512],
                             start=(fc == 0), stop=(fc == 7))
                    inst = t.matmul(ps_sc0[:, 512:1024],
                                    WQKV1[:, 4096 + fc * 512 + hp * 128: 4096 + fc * 512 + (hp + 1) * 128],
                                    LN2FR[:, slot * 4096 + fc * 512: slot * 4096 + (fc + 1) * 512],
                                    start=(fc == 0), stop=(fc == 7))
                bk.inc(inst, "sT", 1)
                EV["p8_qk_mm"] = bk.v("sT")
            TE(p8_qk)

            def p8_qk_ev(v, gqs=gqs, hp=hp):
                bk.w(v, "sT", EV["p8_qk_mm"])
                bk.inc(v.tensor_copy(out=QFM[:, hp * 4096 + gqs * 512: hp * 4096 + (gqs + 1) * 512],
                              in_=ps_sc0[:, 0:512]), "sV", 1)
                inst = v.tensor_copy(out=KFM[:, hp * 4096 + gqs * 512: hp * 4096 + (gqs + 1) * 512],
                                     in_=ps_sc0[:, 512:1024])
                bk.inc(inst, "sV", 1)
                EV["p8_qk_ev"] = bk.v("sV")
            VE(p8_qk_ev)

        for ts4 in range(4):
            def p8_v(t, gqs=gqs, ts4=ts4, slot=slot):
                bk.w(t, EV[f"p8_ld_{gqs}"][0], EV[f"p8_ld_{gqs}"][1])
                if "p8_v_ev" in EV:
                    bk.w(t, "sV", EV["p8_v_ev"])
                for fc in range(8):
                    inst = t.matmul(ps_sc1[:, 0:512],
                                    LN2FR[:, slot * 4096 + fc * 512 + ts4 * 128: slot * 4096 + fc * 512 + (ts4 + 1) * 128],
                                    WQKV1[:, 8192 + fc * 512: 8192 + (fc + 1) * 512],
                                    start=(fc == 0), stop=(fc == 7))
                bk.inc(inst, "sT", 1)
                EV["p8_v_mm"] = bk.v("sT")
            TE(p8_v)

            def p8_v_ev(v, gqs=gqs, ts4=ts4):
                bk.w(v, "sT", EV["p8_v_mm"])
                gts = gqs * 4 + ts4
                if gts < 31:
                    inst = v.tensor_copy(
                        out=vout(gts * 8, 8),
                        in_=ps_sc1[:, 0:512].rearrange("p (c t) -> p c t", t=64))
                    bk.inc(inst, "sV", 1)
                else:
                    bk.inc(v.tensor_copy(
                        out=vout(248, 4),
                        in_=ps_sc1[:, 0:256].rearrange("p (c t) -> p c t", t=64)), "sV", 1)
                    inst = v.tensor_copy(
                        out=vout(252, 4),
                        in_=ps_sc1[:, 256:512].rearrange("p (c t) -> p c t", t=64))
                    bk.inc(inst, "sV", 1)
                EV["p8_v_ev"] = bk.v("sV")
            VE(p8_v_ev)
        EV[f"p8_qkv_{gqs}"] = bk.v("sV")
    EV["p8_done"] = bk.v("sV")

    norms_flat("l1", QFM, KFM, ("sV", EV["p8_done"]))

    def conc_hook(hp, qs):
        if qs != 7:
            return

        def send(g, hp=hp):
            bk.w(g, "sV", EV["att_evac"])
            pid = nc.partition_id([mybir.EngineType.Pool])
            off = g.snap((1 - pid % 2) * 2048 + hp * 4096, min_val=hp * 4096,
                         max_val=hp * 4096 + 2048)
            bk.inc(g.dma_start(out=agc_in.ap()[hp * 128:(hp + 1) * 128, :],
                               in_=CONC[:, bass.ds(off, 2048)]), "d_p10", 16)
            ks = (0, 1) if hp == 2 else ((2, 3) if hp == 3 else ())
            for k in ks:
                ooff = g.snap((pid % 2) * 2048 + k * 4096, min_val=k * 4096,
                              max_val=k * 4096 + 2048)
                bk.inc(g.dma_start(out=CONCO[:, k * 2048:(k + 1) * 2048],
                                   in_=CONC[:, bass.ds(ooff, 2048)]), "d_p10", 16)
        GE(send)

    attention(1, CONC, rrep_mode=True, qs_end=conc_hook)
    if DBG:
        dump(dbg_conc, CONC[:, :])

    def p10_g(g):
        bk.w(g, "sV", EV["attn1_done"])
        bk.w(g, "d_p10")
        pid = nc.partition_id([mybir.EngineType.Pool])
        inst = g.collective_compute("AllGather", OP.bypass, replica_groups=PAIRS,
                                    ins=[agc_in.ap().opt()], outs=[agc_out.ap().opt()])
        bk.inc(inst, "cc", 1)
        bk.w(g, "cc")
        for k in range(4):
            roff = g.snap((1 - pid % 2) * 512 + k * 128, min_val=k * 128,
                          max_val=512 + k * 128)
            bk.inc(g.dma_start(out=CONCR[:, k * 2048:(k + 1) * 2048],
                               in_=agc_out.ap()[bass.ds(roff, 128), :]), "d_p10", 16)
        bk.w(g, "d_p10")
        bk.inc(g.dma_start(out=sb3(WOUT1[:, :], 1024), in_=fm3(wo1)), "d_p10", 16)
        bk.w(g, "d_p10")
        bk.inc(g.nop(), "sG", 1)
        EV["p10_x"] = bk.v("sG")
    GE(p10_g)

    def p10_reload(sy):
        bk.w(sy, "d_xsp", EV["p7_sp"])
        bk.w(sy, "sV", EV["attn1_done"])
        bk.inc(sy.dma_start(out=sb3(STRM[:, :], 2048), in_=fm3(x2sp.ap())), "d_rl", 16)
        EV["p10_rl"] = bk.v("d_rl")
    SY(p10_reload)

    for st in range(4):
        for oc in range(8):
            half = oc % 2
            pb = ps_sc0 if half == 0 else ps_sc1

            def p11_mm(t, st=st, oc=oc, pb=pb, half=half):
                if st == 0 and oc == 0:
                    bk.w(t, "sG", EV["p10_x"])
                if f"p11_ev_h{half}" in EV:
                    bk.w(t, "sV", EV[f"p11_ev_h{half}"])
                for fc in range(8):
                    src = CONCO if fc < 4 else CONCR
                    k = fc if fc < 4 else fc - 4
                    inst = t.matmul(pb[:, 0:512],
                                    WOUT1[:, fc * 1024 + oc * 128: fc * 1024 + (oc + 1) * 128],
                                    src[:, k * 2048 + st * 512: k * 2048 + (st + 1) * 512],
                                    start=(fc == 0), stop=(fc == 7))
                bk.inc(inst, "sT", 1)
                EV["p11_mm"] = bk.v("sT")
            TE(p11_mm)

            def p11_ev(v, st=st, oc=oc, pb=pb, half=half):
                bk.w(v, "sT", EV["p11_mm"])
                if st == 0 and oc == 0:
                    bk.w(v, "d_rl", EV["p10_rl"])
                inst = v.scalar_tensor_tensor(
                    out=STRM[:, oc * 2048 + st * 512: oc * 2048 + (st + 1) * 512],
                    in0=pb[:, 0:512], scalar=BO1[:, oc:oc + 1],
                    in1=STRM[:, oc * 2048 + st * 512: oc * 2048 + (st + 1) * 512],
                    op0=OP.add, op1=OP.add)
                bk.inc(inst, "sV", 1)
                EV[f"p11_ev_h{half}"] = bk.v("sV")
            VE(p11_ev)
    EV["p11_done"] = bk.v("sV")

    if DBG:
        dump(dbg_x3, STRM[:, :])

    def p12_r12(sy):
        bk.w(sy, "sV", EV["p11_done"])
        bk.w(sy, "d_rsp", EV["r12_sp"])
        r3 = r12sp.ap().rearrange("r c p t -> p r c t")
        r12v = R12[:, :].rearrange("p (r c t) -> p r c t", r=2, t=512)
        bk.inc(sy.dma_start(out=r12v, in_=r3), "d_r12", 16)
        EV["p12_r12"] = bk.v("d_r12")
    SY(p12_r12)

    def p12_bfc(g):
        bk.w(g, "sV", EV["p11_done"])
        bk.inc(g.dma_start(out=BFC[:, :], in_=bfc), "d_bfc", 16)
        EV["p12_bfc"] = bk.v("d_bfc")
    GE(p12_bfc)

    for st in range(4):
        def ssrc3(c, st=st):
            return STRM[:, c * 2048 + st * 512: c * 2048 + (st + 1) * 512]
        ln_stats("ln3", st, ssrc3, ("sV", EV["p11_done"]))

        def nv3(v, st=st):
            bk.w(v, "sT", EV[f"ln3_rep_{st}"])
            for c in range(8):
                bk.w(v, "sV")
                bk.chain(v, v.tensor_tensor(out=SCR1[:, :], in0=ssrc3(c),
                                             in1=ps_sc0[:, 0:512], op=OP.mult), "sV")
                bk.inc(v.tensor_tensor(
                    out=LN3[:, c * 2048 + st * 512: c * 2048 + (st + 1) * 512],
                    in0=SCR1[:, :], in1=ps_sc0[:, 512:1024], op=OP.subtract), "sV", 1)
        VE(nv3)
    EV["ln3_done"] = bk.v("sV")

    if DBG:
        dump(dbg_ln3, LN3[:, :])

        def _wr12(sy):
            bk.w(sy, "d_r12", EV["p12_r12"])
        SY(_wr12)
        dump(dbg_r12, R12[:, :])

    for tc in range(16):
        slot = tc % 2
        for grp in range(2):
            def p13_tr(t, tc=tc, grp=grp):
                if tc == 0 and grp == 0:
                    bk.w(t, "sV", EV["ln3_done"])
                if "p13_x3t" in EV:
                    bk.w(t, "sV", EV["p13_x3t"])
                for ci in range(4):
                    c = grp * 4 + ci
                    inst = t.transpose(ps_m1[:, ci * 128:(ci + 1) * 128],
                                       STRM[:, c * 2048 + tc * 128: c * 2048 + (tc + 1) * 128],
                                       IDF[:, :])
                bk.inc(inst, "sT", 1)
                EV["p13_tr"] = bk.v("sT")
            TE(p13_tr)

            def p13_trev(v, tc=tc, grp=grp, slot=slot):
                bk.w(v, "sT", EV["p13_tr"])
                if tc >= 2 and grp == 0:
                    bk.w(v, EV[f"p13_x4_{tc-2}"][0], EV[f"p13_x4_{tc-2}"][1])
                for ci in range(4):
                    c = grp * 4 + ci
                    bk.inc(v.tensor_copy(
                        out=X3TM[:, slot * 1024 + c * 128: slot * 1024 + (c + 1) * 128],
                        in_=ps_m1[:, ci * 128:(ci + 1) * 128]), "sV", 1)
                EV["p13_x3t"] = bk.v("sV")
            VE(p13_trev)
        EV[f"p13_x3t_done_{tc}"] = bk.v("sV")

        for oh in range(2):
            pb = ps_sc0 if oh == 0 else ps_sc1

            def p13_mm(t, tc=tc, oh=oh, pb=pb):
                if tc == 0 and oh == 0:
                    bk.w(t, "d_r12", EV["p12_r12"])
                if f"p13_ev_h{oh}" in EV:
                    bk.w(t, "sV", EV[f"p13_ev_h{oh}"])
                for fc in range(8):
                    inst = t.matmul(pb[:, 0:512],
                                    LN3[:, fc * 2048 + tc * 128: fc * 2048 + (tc + 1) * 128],
                                    R12[:, oh * 4096 + fc * 512: oh * 4096 + (fc + 1) * 512],
                                    start=(fc == 0), stop=(fc == 7))
                bk.inc(inst, "sT", 1)
                EV["p13_mm"] = bk.v("sT")
            TE(p13_mm)

            def p13_ev(v, tc=tc, oh=oh, pb=pb, slot=slot):
                bk.w(v, "sT", EV["p13_mm"])
                if tc == 0 and oh == 0:
                    bk.w(v, "d_bfc", EV["p12_bfc"])
                bk.w(v, "sV")
                bk.chain(v, v.tensor_tensor(out=SCR1[:, :], in0=pb[:, 0:512],
                                            in1=BFC[:, oh * 512:(oh + 1) * 512], op=OP.add), "sV")
                inst = v.tensor_tensor(
                    out=X4T[:, slot * 1024 + oh * 512: slot * 1024 + (oh + 1) * 512],
                    in0=SCR1[:, :],
                    in1=X3TM[:, slot * 1024 + oh * 512: slot * 1024 + (oh + 1) * 512],
                    op=OP.add)
                bk.inc(inst, "sV", 1)
                EV[f"p13_ev_h{oh}"] = bk.v("sV")
            VE(p13_ev)
        EV[f"p13_x4v_{tc}"] = bk.v("sV")

        def p13_out(sy, tc=tc, slot=slot):
            bk.w(sy, "sV", EV[f"p13_x4v_{tc}"])
            sem = "d_oa" if slot == 0 else "d_ob"
            bk.inc(sy.dma_start(out=out[tc * 128:(tc + 1) * 128, :],
                                in_=X4T[:, slot * 1024:(slot + 1) * 1024]), sem, 16)
            EV[f"p13_x4_{tc}"] = (sem, bk.v(sem))
        SY(p13_out)

    def fin(g):
        bk.w(g, "d_oa", bk.v("d_oa"))
        bk.w(g, "d_ob", bk.v("d_ob"))
        g.nop()
    GE(fin)

    stack.close()
    nc.compile()
    return nc


_NC_CACHE = {}


def _get_nc():
    if "nc" not in _NC_CACHE:
        _NC_CACHE["nc"] = build_nc()
    return _NC_CACHE["nc"]


def _host_inputs(c, a):
    b, p = c // 2, c % 2
    bf = ml_dtypes.bfloat16
    x = np.asarray(a["L0_attn_x"])
    xp = np.ascontiguousarray(x[b].T)
    perm = np.concatenate([np.arange(512 * p, 512 * p + 512),
                           np.arange(512 * (1 - p), 512 * (1 - p) + 512)])
    m = {}
    m["x0f"] = np.ascontiguousarray(xp[perm])
    m["x0o"] = np.ascontiguousarray(x[b, p * 2048:(p + 1) * 2048].T)
    m["wq0"] = np.ascontiguousarray(np.asarray(a["L0_attn_Wq"])[8 * p:8 * p + 8])
    m["wk0"] = np.ascontiguousarray(np.asarray(a["L0_attn_Wk"])[8 * p:8 * p + 8])
    m["wv0"] = np.ascontiguousarray(np.asarray(a["L0_attn_Wv"])[8 * p:8 * p + 8])
    m["wo0"] = np.ascontiguousarray(np.asarray(a["L0_attn_W_O"]))
    m["wf0"] = np.ascontiguousarray(np.asarray(a["L0_ffn_weight"]).T)
    m["bf0"] = np.ascontiguousarray(np.asarray(a["L0_ffn_bias"]).reshape(8, 128).T)
    w = np.asarray(a["L1_attn_w_qkv"])
    m["wq1"] = np.ascontiguousarray(w[:, 512 * p:512 * p + 512])
    m["wk1"] = np.ascontiguousarray(w[:, 1024 + 512 * p:1024 + 512 * p + 512])
    m["wv1"] = np.ascontiguousarray(w[:, 2048 + 512 * p:2048 + 512 * p + 512])
    m["wo1"] = np.ascontiguousarray(np.asarray(a["L1_attn_w_out"])[perm])
    m["bo1"] = np.ascontiguousarray(np.asarray(a["L1_attn_b_out"]).reshape(8, 128).T)
    g1r = np.asarray(a["L1_ffn_G1_R"]); g2r = np.asarray(a["L1_ffn_G2_R"])
    g1i = np.asarray(a["L1_ffn_G1_I"]); g2i = np.asarray(a["L1_ffn_G2_I"])
    m["uR"] = np.ascontiguousarray(np.stack(
        [np.repeat(np.repeat(g1r[0, :, ci, :].T, 16, 0), 16, 1) for ci in range(4)]),
        dtype=np.float32)
    m["vR"] = np.ascontiguousarray(np.stack(
        [np.tile(g2r[ci, :, 0, :].T, (32, 32)) for ci in range(4)]), dtype=np.float32)
    m["uI"] = np.ascontiguousarray(np.stack(
        [np.repeat(np.repeat(g1i[0, :, ci, :].T, 16, 0), 16, 1) for ci in range(4)]),
        dtype=np.float32)
    m["vI"] = np.ascontiguousarray(np.stack(
        [np.tile(g2i[ci, :, 0, :].T, (32, 32)) for ci in range(4)]), dtype=np.float32)
    m["bfc"] = np.ascontiguousarray(np.tile(np.concatenate(
        [np.asarray(a["L1_ffn_bias_real"]), np.asarray(a["L1_ffn_bias_imag"])])[None, :],
        (128, 1)).astype(np.float32))
    m["idb"] = np.eye(128, dtype=np.float32).astype(bf)
    m["idf"] = np.eye(128, dtype=np.float32)
    yy, xx = np.mgrid[0:128, 0:896]
    m["mbig"] = ((xx - 384) >= yy).astype(np.float32).astype(bf)
    sh = np.zeros((128, 4), np.float32)
    sh[0:64, 0] = 1.0
    sh[64:128, 1] = 1.0
    m["selh"] = sh.astype(bf)
    cstv = np.zeros((128, 8), np.float32)
    cstv[:, 0] = 1e-12
    cstv[:, 1] = 16.0
    cstv[:, 4] = 1e-5
    m["cst"] = cstv
    return m


def kernel(**inputs):
    nc = _get_nc()
    in_maps = [_host_inputs(c, inputs) for c in range(NCORES)]
    res = run_bass_kernel_spmd(nc, in_maps, core_ids=list(range(NCORES)))
    outp = np.empty((B, S, E), dtype=np.float32)
    for c in range(NCORES):
        b, p = c // 2, c % 2
        outp[b, p * 2048:(p + 1) * 2048, :] = res.results[c]["out"]
    return outp
